# version 05
# baseline (speedup 1.0000x reference)
import numpy as np
import concourse.bass as bass
import concourse.mybir as mybir
from concourse.bass_utils import run_bass_kernel_spmd
from contextlib import ExitStack

F32 = mybir.dt.float32
BF16 = mybir.dt.bfloat16
F8E4 = mybir.dt.float8e4
ALU = mybir.AluOpType
AF = mybir.ActivationFunctionType
AX = mybir.AxisListType

D = 2048
TW = 2048
TO = 1024
II = 8192
NJ = D // 128
NTW = TW // 128
NTO = TO // 128
NIC = II // 128


def ts(i, n=128):
    return slice(i * n, (i + 1) * n)


class Sem:
    def __init__(self, nc, ctx, name):
        self.h = ctx.enter_context(nc.semaphore(name))
        self.n = 0


class Prog:
    ENGS = ("sync", "scalar", "vector", "tensor", "gpsimd")

    def __init__(self, nc, ctx):
        self.nc = nc
        self.ctx = ctx
        self.q = {e: [] for e in self.ENGS}
        self.sems = {e: Sem(nc, ctx, f"sem_{e}") for e in self.ENGS}

    def dsem(self, name):
        if name not in self.sems:
            self.sems[name] = Sem(self.nc, self.ctx, f"sem_{name}")
        return self.sems[name]

    def total(self, name):
        s = self.dsem(name)
        return (s, s.n)

    def dma(self, eng, out, in_, sem_name, wait=None, slow=False, accum=None):
        s = self.dsem(sem_name)
        s.n += 16
        tgt = s.n
        w = list(wait or [])

        def f(E, out=out, in_=in_, w=w, s=s, slow=slow, accum=accum):
            for (ws, wv) in w:
                E.wait_ge(ws.h, wv)
            if accum is not None:
                E.dma_start(out=out, in_=in_, accum_op=accum,
                            allow_slow_non_contiguous=slow).then_inc(s.h, 16)
            else:
                E.dma_start(out=out, in_=in_,
                            allow_slow_non_contiguous=slow).then_inc(s.h, 16)
        self.q[eng].append(f)
        return (s, tgt)

    def op(self, eng, fn, wait=None):
        s = self.sems[eng]
        s.n += 1
        w = list(wait or [])

        def f(E, fn=fn, w=w, s=s):
            for (ws, wv) in w:
                E.wait_ge(ws.h, wv)
            r = fn(E)
            if isinstance(r, list):
                r = r[-1]
            r.then_inc(s.h, 1)
        self.q[eng].append(f)
        return (s, tgt_of(s))

    def wait_only(self, eng, wait):
        w = list(wait)

        def f(E, w=w):
            for (ws, wv) in w:
                E.wait_ge(ws.h, wv)
        self.q[eng].append(f)

    def bar(self, eng):
        s = self.sems[eng]
        return (s, s.n)

    def install(self, block):
        for e in self.ENGS:
            thunks = self.q[e]

            def mk(thunks):
                def body(E):
                    for t in thunks:
                        t(E)
                return body
            getattr(block, e)(mk(thunks))


def tgt_of(s):
    return s.n


def build(n_phases=5, debug=False):
    nc = bass.Bass("TRN2", target_bir_lowering=False)
    x_win = nc.dram_tensor("x_win", [TW, D], F32, kind="ExternalInput").ap()
    bout_tile = nc.dram_tensor("bout_tile", [128, D], F32, kind="ExternalInput").ap()
    ident = nc.dram_tensor("ident", [128, 128], BF16, kind="ExternalInput").ap()
    w_in = nc.dram_tensor("w_in", [D, D], F32, kind="ExternalInput").ap()
    w_dec = nc.dram_tensor("w_dec", [D, D], F32, kind="ExternalInput").ap()
    w_gate = nc.dram_tensor("w_gate", [D, D], F32, kind="ExternalInput").ap()
    w_out = nc.dram_tensor("w_out", [D, D], F32, kind="ExternalInput").ap()
    b_in = nc.dram_tensor("b_in", [D], F32, kind="ExternalInput").ap()
    b_dec = nc.dram_tensor("b_dec", [D], F32, kind="ExternalInput").ap()
    b_gate = nc.dram_tensor("b_gate", [D], F32, kind="ExternalInput").ap()
    b_out = nc.dram_tensor("b_out", [D], F32, kind="ExternalInput").ap()
    dec_bias = nc.dram_tensor("dec_bias", [D], F32, kind="ExternalInput").ap()
    bmask = nc.dram_tensor("bmask", [128, 1], F32, kind="ExternalInput").ap()
    gw_nat = nc.dram_tensor("gw_nat", [II, D], F32, kind="ExternalInput").ap()
    uw_nat = nc.dram_tensor("uw_nat", [II, D], F32, kind="ExternalInput").ap()
    dw_nat = nc.dram_tensor("dw_nat", [D, II], F32, kind="ExternalInput").ap()
    out_tm = nc.dram_tensor("out_tm", [TO, D], F32, kind="ExternalOutput").ap()
    s_scr = nc.dram_tensor("s_scr", [TO, D], F32, kind="Internal").ap()
    gwq = nc.dram_tensor("gwq", [D, II], BF16, kind="Internal").ap()
    uwq = nc.dram_tensor("uwq", [D, II], BF16, kind="Internal").ap()
    dwq = nc.dram_tensor("dwq", [II, D], BF16, kind="Internal").ap()
    dbg = {}
    if debug:
        dbg["xn_fm"] = nc.dram_tensor("dbg_xn_fm", [NJ, 128, TW], F32, kind="ExternalOutput").ap()
        dbg["xq_fm"] = nc.dram_tensor("dbg_xq_fm", [NJ, 128, TO], F32, kind="ExternalOutput").ap()
        dbg["s_fm"] = nc.dram_tensor("dbg_s_fm", [TO, D], F32, kind="ExternalOutput").ap()
        dbg["hq"] = nc.dram_tensor("dbg_hq", [NIC, 128, TO], F32, kind="ExternalOutput").ap()
        dbg["gwq"] = nc.dram_tensor("dbg_gwq", [256, II], BF16, kind="ExternalOutput").ap()
        dbg["dwq"] = nc.dram_tensor("dbg_dwq", [256, D], BF16, kind="ExternalOutput").ap()

    with ExitStack() as ctx:
        S = lambda *a: ctx.enter_context(nc.sbuf_tensor(*a))
        PS = lambda *a: ctx.enter_context(nc.psum_tensor(*a))
        xn_fm = S("xn_fm", [128, NJ * TW], BF16)
        xq_fm = S("xq_fm", [128, NJ * TO], BF16)
        y_fm = S("y_fm", [128, NJ * TO], BF16)
        SCR = S("SCR", [128, 19840], F32)
        stat = S("stat", [128, 16], F32)
        bias_c = S("bias_c", [128, 96], F32)
        idt = S("idt", [128, 128], BF16)
        qsc = S("qsc", [128, 64], F32)
        qs2 = S("qs2", [128, 64], F32)
        qrr = S("qrr", [128, 64], F32)
        pA = PS("pA", [128, 2048], F32)
        pB = PS("pB", [128, 2048], F32)
        pBb = pB[:].bitcast(BF16)
        pTa = pBb[:, 0:1024]
        pTb = pBb[:, 1024:2048]

        hq = lambda c: xn_fm[:, c * 512:(c + 1) * 512]

        p = Prog(nc, ctx)
        w_id = p.dma("sync", idt[:], ident, "ld_id")

        xw = [SCR[:, 0:2048], SCR[:, 2048:4096]]
        xnbf = [SCR[:, 4096:5120].bitcast(BF16), SCR[:, 5120:6144].bitcast(BF16)]
        junk = SCR[:, 6144:7168].bitcast(BF16)
        q8 = SCR[:, 7168:7680].bitcast(F8E4)
        xqbf = [SCR[:, 7680:8704].bitcast(BF16), SCR[:, 8704:9728].bitcast(BF16)]

        tr_done = [((p.sems["scalar"]), 0)]
        trq_done = [((p.sems["scalar"]), 0)]
        for t in range(NTW):
            buf = xw[t % 2]
            nb = xnbf[t % 2]
            own = t >= NTW - NTO
            wait_reuse = [tr_done[-2]] if len(tr_done) >= 2 else []
            w_x = p.dma("sync", buf, x_win[ts(t), :], f"x{t % 2}", wait=wait_reuse)
            w_sum = p.op("vector", lambda E, buf=buf: E.tensor_reduce(
                out=stat[:, 0:1], in_=buf, axis=AX.X, op=ALU.add), wait=[w_x])
            w_nm = p.op("vector", lambda E: E.tensor_scalar(
                out=stat[:, 1:2], in0=stat[:, 0:1], scalar1=1.0 / D, scalar2=-1.0,
                op0=ALU.mult, op1=ALU.mult), wait=[w_sum])
            w_pm = p.op("vector", lambda E: E.tensor_scalar(
                out=stat[:, 2:3], in0=stat[:, 0:1], scalar1=1.0 / D, scalar2=None,
                op0=ALU.mult), wait=[w_sum])
            w_sq = p.op("scalar", lambda E, buf=buf: E.activation(
                out=junk, in_=buf, func=AF.Square, bias=stat[:, 1:2],
                accum_out=stat[:, 3:4]), wait=[w_nm])
            w_var = p.op("vector", lambda E: E.tensor_scalar(
                out=stat[:, 4:5], in0=stat[:, 3:4], scalar1=1.0 / D, scalar2=1e-5,
                op0=ALU.mult, op1=ALU.add), wait=[w_sq])
            w_sqr = p.op("scalar", lambda E: E.activation(
                out=stat[:, 5:6], in_=stat[:, 4:5], func=AF.Sqrt), wait=[w_var])
            w_rstd = p.op("vector", lambda E: E.reciprocal(
                out=stat[:, 6:7], in_=stat[:, 5:6]), wait=[w_sqr])
            if own:
                to = t - (NTW - NTO)
                xn32 = SCR[:, 9728:11776]
                w_xn32 = p.op("vector", lambda E, buf=buf, xn32=xn32: E.tensor_scalar(
                    out=xn32, in0=buf, scalar1=stat[:, 2:3], scalar2=stat[:, 6:7],
                    op0=ALU.subtract, op1=ALU.mult), wait=[w_rstd])
                w_xnbf = p.op("vector", lambda E, nb=nb, xn32=xn32: E.tensor_copy(
                    nb, xn32), wait=[w_xn32])
            else:
                w_xnbf = p.op("vector", lambda E, buf=buf, nb=nb: E.tensor_scalar(
                    out=nb, in0=buf, scalar1=stat[:, 2:3], scalar2=stat[:, 6:7],
                    op0=ALU.subtract, op1=ALU.mult), wait=[w_rstd])
            last = None
            for jg in range(2):
                sc_bar = p.bar("scalar")
                w_tr = p.op("tensor", lambda E, nb=nb, jg=jg: [
                    E.transpose(pTa[:, 128 * k:128 * (k + 1)], nb[:, ts(8 * jg + k)], idt[:])
                    for k in range(8)], wait=[w_xnbf, w_id, sc_bar])
                w_ev = p.op("scalar", lambda E, t=t, jg=jg: [
                    E.copy(xn_fm[:, (8 * jg + k) * TW + t * 128:(8 * jg + k) * TW + (t + 1) * 128],
                           pTa[:, 128 * k:128 * (k + 1)]) for k in range(8)], wait=[w_tr])
                last = w_ev
            tr_done.append(last)
            if own:
                to = t - (NTW - NTO)
                qb = xqbf[t % 2]
                a3 = xn32.rearrange("p (b c) -> p b c", c=32)
                w_red = p.op("vector", lambda E, a3=a3: E.tensor_reduce(
                    out=qsc[:], in_=a3, axis=AX.X, op=ALU.max,
                    apply_absolute_value=True), wait=[w_xn32])
                w_s2 = p.op("vector", lambda E: E.tensor_scalar(
                    out=qs2[:], in0=qsc[:], scalar1=1.0 / 224.0, scalar2=2e-30,
                    op0=ALU.mult, op1=ALU.max), wait=[w_red])
                w_r = p.op("vector", lambda E: E.reciprocal(out=qrr[:], in_=qs2[:]),
                           wait=[w_s2])
                rb = qrr[:].unsqueeze(2).broadcast_to([128, 64, 32])
                s2b = qs2[:].unsqueeze(2).broadcast_to([128, 64, 32])
                wait_reuse = [trq_done[-2]] if len(trq_done) >= 2 else []
                w_q = p.op("vector", lambda E, a3=a3, rb=rb: E.tensor_tensor(
                    out=q8.rearrange("p (b c) -> p b c", c=32), in0=a3, in1=rb,
                    op=ALU.mult), wait=[w_r] + wait_reuse)
                w_dq = p.op("vector", lambda E, qb=qb, s2b=s2b: E.tensor_tensor(
                    out=qb.rearrange("p (b c) -> p b c", c=32),
                    in0=q8.rearrange("p (b c) -> p b c", c=32), in1=s2b,
                    op=ALU.mult), wait=[w_q])
                last = None
                for jg in range(2):
                    sc_bar = p.bar("scalar")
                    w_tr = p.op("tensor", lambda E, qb=qb, jg=jg: [
                        E.transpose(pTa[:, 128 * k:128 * (k + 1)], qb[:, ts(8 * jg + k)], idt[:])
                        for k in range(8)], wait=[w_dq, sc_bar])
                    w_ev = p.op("scalar", lambda E, to=to, jg=jg: [
                        E.copy(xq_fm[:, (8 * jg + k) * TO + to * 128:(8 * jg + k) * TO + (to + 1) * 128],
                               pTa[:, 128 * k:128 * (k + 1)]) for k in range(8)], wait=[w_tr])
                    last = w_ev
                trq_done.append(last)

        A_done_sc = p.bar("scalar")
        A_done_ve = p.bar("vector")

        if n_phases >= 2:
            bi = lambda i: bias_c[:, 16 * i:16 * (i + 1)]
            p.dma("sync", bi(0), b_in.rearrange("(j p) -> p j", p=128), "ld_bias", slow=True)
            p.dma("sync", bi(1), b_dec.rearrange("(j p) -> p j", p=128), "ld_bias", slow=True)
            p.dma("sync", bi(3), b_gate.rearrange("(j p) -> p j", p=128), "ld_bias", slow=True)
            p.dma("sync", bias_c[:, 80:96], dec_bias.rearrange("(j p) -> p j", p=128), "ld_bias", slow=True)
            p.dma("sync", stat[:, 7:8], bmask, "ld_bias")
            w_b1 = w_b2 = w_b3 = w_b5 = w_bm = p.total("ld_bias")
            w_bd = p.op("vector", lambda E: E.tensor_tensor(
                out=bi(1), in0=bi(1), in1=bias_c[:, 80:96], op=ALU.add),
                wait=[w_b2, w_b5])
            w_bn = p.op("vector", lambda E: E.tensor_scalar(
                out=bi(2), in0=bi(1), scalar1=-1.0, scalar2=None, op0=ALU.mult),
                wait=[w_bd])

            panF = [SCR[:, 0:2048], SCR[:, 2048:4096]]
            abuf = SCR[:, 4096:6144]
            ibuf = SCR[:, 6144:8192]
            bbuf = SCR[:, 8192:10240]
            gbuf = SCR[:, 10240:11264]
            s_buf = SCR[:, 11264:12288]
            panW = [SCR[:, 11648 + 1024 * i:11648 + 1024 * (i + 1)].bitcast(BF16)
                    for i in range(6)]
            panO = [SCR[:, 17792 + 1024 * i:17792 + 1024 * (i + 1)].bitcast(BF16)
                    for i in range(2)]

            pan_ready = {}
            panw_prev = {}
            panf_prev = {}

            def prefetch(j):
                for wi, wten in enumerate((w_in, w_dec, w_gate)):
                    fs = panF[wi % 2]
                    dst = panW[2 * wi + (j % 2)]
                    wx = []
                    if (2 * wi + (j % 2)) in panw_prev:
                        wx.append(panw_prev[2 * wi + (j % 2)])
                    key = (wi % 2)
                    wfs = [panf_prev[key]] if key in panf_prev else []
                    src = wten.rearrange("(k p) e -> p k e", p=128)[:, :, ts(j)]
                    if j == 0 and wi == 0:
                        wx = wx + [A_done_sc, A_done_ve]
                    w_l = p.dma("sync", fs, src, f"pf{wi % 2}", wait=wx + wfs)
                    w_c = p.op("vector", lambda E, fs=fs, dst=dst: E.tensor_copy(dst, fs),
                               wait=[w_l])
                    panf_prev[key] = w_c
                    pan_ready[(wi, j)] = w_c

            prefetch(0)
            w_y_all = []
            for j in range(NJ):
                if j + 1 < NJ:
                    prefetch(j + 1)
                pin = panW[0 + (j % 2)]
                pdec = panW[2 + (j % 2)]
                pgate = panW[4 + (j % 2)]
                sc_bar = p.bar("scalar")
                ve_bar = p.bar("vector")
                w_mm_in = p.op("tensor", lambda E, pin=pin: [
                    E.matmul(pA[:, 512 * t4:512 * (t4 + 1)], pin[:, ts(k)],
                             xn_fm[:, k * TW + 512 * t4:k * TW + 512 * (t4 + 1)],
                             start=(k == 0), stop=(k == NJ - 1))
                    for k in range(NJ) for t4 in range(4)],
                    wait=[pan_ready[(0, j)], sc_bar, ve_bar])
                w_mm_dec = p.op("tensor", lambda E, pdec=pdec: [
                    E.matmul(pB[:, 512 * t4:512 * (t4 + 1)], pdec[:, ts(k)],
                             xn_fm[:, k * TW + 512 * t4:k * TW + 512 * (t4 + 1)],
                             start=(k == 0), stop=(k == NJ - 1))
                    for k in range(NJ) for t4 in range(4)],
                    wait=[pan_ready[(1, j)]])
                ev_in = p.op("scalar", lambda E, j=j: E.activation(
                    out=ibuf, in_=pA[:], func=AF.Silu, bias=bi(0)[:, j:j + 1]),
                    wait=[w_mm_in, w_b1])
                ev_a = p.op("scalar", lambda E, j=j: E.activation(
                    out=abuf, in_=pB[:], func=AF.Sigmoid, bias=bi(1)[:, j:j + 1]),
                    wait=[w_mm_dec, w_bd])
                ev_oma = p.op("scalar", lambda E, j=j: E.activation(
                    out=bbuf, in_=pB[:], func=AF.Sigmoid, bias=bi(2)[:, j:j + 1],
                    scale=-1.0), wait=[w_bn])
                w_mm_g = p.op("tensor", lambda E, pgate=pgate: [
                    E.matmul(pA[:, 512 * t4:512 * (t4 + 1)], pgate[:, ts(k)],
                             xn_fm[:, k * TW + TO + 512 * t4:k * TW + TO + 512 * (t4 + 1)],
                             start=(k == 0), stop=(k == NJ - 1))
                    for k in range(NJ) for t4 in range(2)],
                    wait=[pan_ready[(2, j)], ev_in])
                panw_prev[0 + (j % 2)] = ev_in
                panw_prev[2 + (j % 2)] = ev_a
                panw_prev[4 + (j % 2)] = w_mm_g
                w_b = p.op("vector", lambda E: E.tensor_tensor(
                    out=bbuf, in0=bbuf, in1=ibuf, op=ALU.mult),
                    wait=[ev_in, ev_oma, w_bm])
                w_bmk = p.op("vector", lambda E: E.tensor_scalar(
                    out=bbuf[:, 0:TO], in0=bbuf[:, 0:TO], scalar1=stat[:, 7:8],
                    scalar2=None, op0=ALU.mult), wait=[w_b])
                w_scan = p.op("vector", lambda E: E.tensor_tensor_scan(
                    out=ibuf, data0=abuf, data1=bbuf, initial=0.0,
                    op0=ALU.mult, op1=ALU.add), wait=[w_bmk, ev_a])
                ev_gate = p.op("scalar", lambda E, j=j: E.activation(
                    out=gbuf, in_=pA[:, 0:TO], func=AF.Sigmoid, bias=bi(3)[:, j:j + 1]),
                    wait=[w_mm_g, w_b3])
                w_y = p.op("vector", lambda E, j=j: E.tensor_tensor(
                    out=y_fm[:, j * TO:(j + 1) * TO], in0=gbuf, in1=ibuf[:, TO:TW],
                    op=ALU.mult), wait=[w_scan, ev_gate])
                w_y_all.append(w_y)

            te_bar_c1 = p.bar("tensor")
            WoutBF = lambda k: xn_fm[:, k * 2048:(k + 1) * 2048]
            bout_t = SCR[:, 4096:6144]
            s_buf2 = SCR[:, 6144:8192]
            ve_bar0 = p.bar("vector")
            w_bt = p.dma("sync", bout_t, bout_tile, "ld_bout", wait=[ve_bar0])
            wo_ready = []
            for k in range(NJ):
                fs = panF[k % 2]
                wfs = [panf_prev[k % 2]] if (k % 2) in panf_prev else []
                w_l = p.dma("sync", fs, w_out[ts(k), :], f"pf{k % 2}", wait=wfs)
                w_c = p.op("vector", lambda E, fs=fs, k=k: E.tensor_copy(
                    WoutBF(k), fs), wait=[w_l] + ([te_bar_c1, w_y_all[-1]] if k < 2 else []))
                panf_prev[k % 2] = w_c
                wo_ready.append(w_c)
            s_dma = []
            for m in range(8):
                ps = pA if m % 2 == 0 else pB
                ve_bar = p.bar("vector")
                sc_bar = p.bar("scalar")
                w_mm_o = p.op("tensor", lambda E, m=m, ps=ps: [
                    E.matmul(ps[:, 512 * t4:512 * (t4 + 1)],
                             y_fm[:, k * TO + m * 128:k * TO + (m + 1) * 128],
                             WoutBF(k)[:, 512 * t4:512 * (t4 + 1)],
                             start=(k == 0), stop=(k == NJ - 1))
                    for k in range(NJ) for t4 in range(4)],
                    wait=[wo_ready[-1], w_y_all[-1], ve_bar, sc_bar])
                w_s = p.op("vector", lambda E, ps=ps: E.tensor_tensor(
                    out=s_buf2, in0=ps[:], in1=bout_t, op=ALU.add),
                    wait=[w_mm_o, w_bt])
                w_sd = p.dma("gpsimd", s_scr[m * 128:(m + 1) * 128, :], s_buf2, "sw",
                             wait=[w_s])
                p.wait_only("vector", [w_sd])
                s_dma.append(w_sd)

        if n_phases >= 3:
            natS = [SCR[:, 0:4096], SCR[:, 4096:8192]]
            deqBB = [[SCR[:, 8192 + 4096 * s + 1024 * r:8192 + 4096 * s + 1024 * (r + 1)].bitcast(BF16)
                      for r in range(4)] for s in range(2)]
            wstg = [SCR[:, 16384 + 256 * i:16384 + 256 * (i + 1)].bitcast(BF16)
                    for i in range(8)]
            qscW = qsc[:]
            qs2W = qs2[:]
            qrW = qrr[:]
            q8W = SCR[:, 16896:17408].bitcast(F8E4)
            nat_prev = {}
            wstg_prev = {}
            deq_guard = {}
            blkcc_i = [0]

            def prep_weight(nat, out, R, C, tag):
                nch = C // 2048
                natv = nat.rearrange("(kk p) c -> p kk c", p=128)
                for blk in range(R // 512):
                    for cc in range(nch):
                        bci = blkcc_i[0]
                        blkcc_i[0] += 1
                        deqB = deqBB[0]
                        deq_ready = []
                        for r in range(4):
                            fs = natS[(r // 2) % 2][:, 2048 * (r % 2):2048 * (r % 2 + 1)]
                            if r % 2 == 0:
                                rp = r // 2
                                slot = rp % 2
                                wx = [nat_prev[slot]] if slot in nat_prev else []
                                if blk == 0 and cc == 0 and tag == "g" and n_phases >= 2:
                                    wx += [p.bar("scalar"), p.bar("vector")]
                                w_l = p.dma("sync", natS[slot],
                                            natv[:, 4 * blk + 2 * rp:4 * blk + 2 * rp + 2,
                                                 cc * 2048:(cc + 1) * 2048],
                                            f"pf{slot}", wait=wx)
                            a3 = fs.rearrange("p (b c) -> p b c", c=32)
                            w_red = p.op("vector", lambda E, a3=a3: E.tensor_reduce(
                                out=qscW, in_=a3, axis=AX.X, op=ALU.max,
                                apply_absolute_value=True), wait=[w_l])
                            w_s2 = p.op("vector", lambda E: E.tensor_scalar(
                                out=qs2W, in0=qscW, scalar1=1.0 / 224.0, scalar2=2e-30,
                                op0=ALU.mult, op1=ALU.max), wait=[w_red])
                            w_r = p.op("vector", lambda E: E.reciprocal(
                                out=qrW, in_=qs2W), wait=[w_s2])
                            rb = qrW.unsqueeze(2).broadcast_to([128, 64, 32])
                            s2b = qs2W.unsqueeze(2).broadcast_to([128, 64, 32])
                            wdq = [deq_guard[0]] if (r == 0 and 0 in deq_guard) else []
                            w_q = p.op("vector", lambda E, a3=a3, rb=rb: E.tensor_tensor(
                                out=q8W.rearrange("p (b c) -> p b c", c=32), in0=a3,
                                in1=rb, op=ALU.mult), wait=[w_r] + wdq)
                            if r % 2 == 1:
                                nat_prev[(r // 2) % 2] = w_q
                            w_dq = p.op("vector", lambda E, r=r, s2b=s2b: E.tensor_tensor(
                                out=deqB[r].rearrange("p (b c) -> p b c", c=32),
                                in0=q8W.rearrange("p (b c) -> p b c", c=32), in1=s2b,
                                op=ALU.mult), wait=[w_q])
                            deq_ready.append(w_dq)
                        for hc in range(16):
                            pt = pTa if hc % 2 == 0 else pTb
                            sc_bar = p.bar("scalar")
                            w_tr = p.op("tensor", lambda E, hc=hc, pt=pt, deqB=deqB: [
                                E.transpose(pt[:, 128 * r:128 * (r + 1)],
                                            deqB[r][:, ts(hc)], idt[:])
                                for r in range(4)], wait=deq_ready + [sc_bar])
                            wg = [wstg_prev[hc % 2]] if (hc % 2) in wstg_prev else []
                            w_ev = p.op("scalar", lambda E, hc=hc, pt=pt: E.copy(
                                wstg[hc % 2][:], pt[:, 0:512]), wait=[w_tr] + wg)
                            w_wr = p.dma("gpsimd", out[cc * 2048 + hc * 128:cc * 2048 + (hc + 1) * 128,
                                                      blk * 512:(blk + 1) * 512],
                                         wstg[hc % 2][:], f"ws{hc % 2}", wait=[w_ev])
                            wstg_prev[hc % 2] = w_wr
                        deq_guard[0] = w_tr
            prep_weight(gw_nat, gwq, II, D, "g")
            prep_weight(uw_nat, uwq, II, D, "u")
            prep_weight(dw_nat, dwq, D, II, "d")
            W_done = [p.total("ws0"), p.total("ws1")]

        if n_phases >= 4:
            panE = [y_fm[:, 0:8192], y_fm[:, 8192:16384]]
            panE2 = [SCR[:, 1600 + 4096 * i:1600 + 4096 * (i + 1)].bitcast(BF16)
                     for i in range(2)]
            gsil = SCR[:, 0:512]
            hbuf = SCR[:, 512:1024]
            q8e = SCR[:, 1024:1152].bitcast(F8E4)
            hqb = SCR[:, 1152:1408].bitcast(BF16)
            qscE = SCR[:, 1536:1552]
            qs2E = SCR[:, 1552:1568]
            qrE = SCR[:, 1568:1584]
            panE_prev = {}
            dpanF = [SCR[:, 9792 + 2048 * i:9792 + 2048 * (i + 1)].bitcast(BF16)
                     for i in range(2)]
            gfs2 = [SCR[:, 13888:14912], SCR[:, 14912:15936]]
            sfs = SCR[:, 15936:16960]
            dpan_prev = {}
            gfs_prev = {}

            def pan_pair(i5):
                return (panE[0], panE[1]) if i5 % 2 == 0 else (panE2[0], panE2[1])

            def prefetch_e(th, i5):
                gp, up = pan_pair(i5)
                key = i5 % 2
                wx = [panE_prev[key]] if key in panE_prev else []
                if th == 0 and i5 < 2:
                    wx = wx + [p.bar("scalar"), p.bar("vector")] + W_done
                w_g = p.dma("sync", gp, gwq.rearrange("(k p) i -> p k i", p=128)[:, :, 512 * i5:512 * (i5 + 1)],
                            f"eg{i5 % 2}", wait=wx)
                w_u = p.dma("sync", up, uwq.rearrange("(k p) i -> p k i", p=128)[:, :, 512 * i5:512 * (i5 + 1)],
                            f"eu{i5 % 2}", wait=[])
                return (w_g, w_u)

            hq_evs = []
            hq_tr_prev = [None]
            for th in range(2):
                pr = prefetch_e(th, 0)
                e_ready = {0: pr}
                mm_prev = {}
                for i5 in range(16):
                    if i5 + 1 < 16:
                        e_ready[i5 + 1] = prefetch_e(th, i5 + 1)
                    gp, up = pan_pair(i5)
                    for m in range(4):
                        half = (m % 2)
                        psg = pA[:, 1024 * half:1024 * half + 512]
                        psu = pA[:, 1024 * half + 512:1024 * half + 1024]
                        sc_bar = p.bar("scalar")
                        ve_bar = p.bar("vector")
                        w_mm = p.op("tensor", lambda E, gp=gp, up=up, th=th, m=m, psg=psg, psu=psu: [
                            E.matmul(pg_ps, lhsT, rhs, start=(k == 0), stop=(k == NJ - 1))
                            for k in range(NJ)
                            for (pg_ps, lhsT, rhs) in (
                                (psg, xq_fm[:, k * TO + th * 512 + m * 128:k * TO + th * 512 + (m + 1) * 128],
                                 gp[:, 512 * k:512 * (k + 1)]),
                                (psu, xq_fm[:, k * TO + th * 512 + m * 128:k * TO + th * 512 + (m + 1) * 128],
                                 up[:, 512 * k:512 * (k + 1)]),
                            )],
                            wait=list(e_ready[i5]) + [sc_bar, ve_bar])
                        mm_prev[half] = w_mm
                        ev_sil = p.op("scalar", lambda E, psg=psg: E.activation(
                            out=gsil, in_=psg, func=AF.Silu), wait=[w_mm])
                        w_h = p.op("vector", lambda E, psu=psu: E.tensor_tensor(
                            out=hbuf, in0=gsil, in1=psu, op=ALU.mult), wait=[ev_sil])
                        a3 = hbuf.rearrange("p (b c) -> p b c", c=32)
                        w_red = p.op("vector", lambda E, a3=a3: E.tensor_reduce(
                            out=qscE, in_=a3, axis=AX.X, op=ALU.max,
                            apply_absolute_value=True), wait=[w_h])
                        w_s2 = p.op("vector", lambda E: E.tensor_scalar(
                            out=qs2E, in0=qscE, scalar1=1.0 / 224.0, scalar2=2e-30,
                            op0=ALU.mult, op1=ALU.max), wait=[w_red])
                        w_r = p.op("vector", lambda E: E.reciprocal(
                            out=qrE, in_=qs2E), wait=[w_s2])
                        rb = qrE.unsqueeze(2).broadcast_to([128, 16, 32])
                        s2b = qs2E.unsqueeze(2).broadcast_to([128, 16, 32])
                        w_q = p.op("vector", lambda E, a3=a3, rb=rb: E.tensor_tensor(
                            out=q8e.rearrange("p (b c) -> p b c", c=32), in0=a3,
                            in1=rb, op=ALU.mult),
                            wait=[w_r] + ([hq_tr_prev[0]] if hq_tr_prev[0] else []))
                        w_dq = p.op("vector", lambda E, s2b=s2b: E.tensor_tensor(
                            out=hqb.rearrange("p (b c) -> p b c", c=32),
                            in0=q8e.rearrange("p (b c) -> p b c", c=32), in1=s2b,
                            op=ALU.mult), wait=[w_q])
                        pt = pTa if (i5 * 4 + m) % 2 == 0 else pTb
                        sc_bar2 = p.bar("scalar")
                        w_tr = p.op("tensor", lambda E, pt=pt: [
                            E.transpose(pt[:, 128 * t:128 * (t + 1)], hqb[:, ts(t)], idt[:])
                            for t in range(4)], wait=[w_dq, sc_bar2])
                        w_ev = p.op("scalar", lambda E, i5=i5, m=m, th=th, pt=pt: [
                            E.copy(hq(i5 * 4 + t)[:, m * 128:(m + 1) * 128],
                                   pt[:, 128 * t:128 * (t + 1)]) for t in range(4)],
                            wait=[w_tr])
                        hq_evs.append(w_ev)
                    panE_prev[i5 % 2] = w_ev
                E_done = [p.bar("scalar"), p.bar("vector")]

                if n_phases >= 5:
                    dwqv = dwq.rearrange("(kk p) h -> p kk h", p=128)
                    for Hh in range(2):
                        mm_last = None
                        for kb in range(16):
                            fs = dpanF[kb % 2]
                            wx = [dpan_prev[kb % 2]] if (kb % 2) in dpan_prev else []
                            if th == 0 and Hh == 0 and kb < 2:
                                wx = wx + E_done + [p.total("sw")]
                            w_l = p.dma("sync", fs,
                                        dwqv[:, 4 * kb:4 * kb + 4,
                                             1024 * Hh:1024 * (Hh + 1)],
                                        f"dp{kb % 2}", wait=wx)
                            bars = ([p.bar("vector"), p.bar("scalar")] if kb == 0 else [])
                            w_mm = p.op("tensor", lambda E, fs=fs, kb=kb, th=th: [
                                E.matmul((pA if m < 2 else pB)[:, 1024 * (m % 2) + 512 * t2:
                                                               1024 * (m % 2) + 512 * (t2 + 1)],
                                         hq(4 * kb + ks)[:, m * 128:(m + 1) * 128],
                                         fs[:, 1024 * ks + 512 * t2:1024 * ks + 512 * (t2 + 1)],
                                         start=(kb == 0 and ks == 0), stop=(kb == 15 and ks == 3))
                                for ks in range(4) for m in range(4) for t2 in range(2)],
                                wait=[w_l, hq_evs[-1]] + bars)
                            dpan_prev[kb % 2] = w_mm
                            mm_last = w_mm
                        for m in range(4):
                            mg = th * 4 + m
                            gi = m % 2
                            pset = (pA if m < 2 else pB)[:, 1024 * (m % 2):1024 * (m % 2 + 1)]
                            gbuf2 = gfs2[gi]
                            wg_re = [gfs_prev[gi]] if gi in gfs_prev else []
                            ev_g = p.op("scalar", lambda E, pset=pset, gbuf2=gbuf2: E.activation(
                                out=gbuf2, in_=pset, func=AF.Silu),
                                wait=[mm_last] + wg_re)
                            w_ls = p.dma("sync", sfs,
                                         s_scr[mg * 128:(mg + 1) * 128,
                                               1024 * Hh:1024 * (Hh + 1)],
                                         "fs", wait=[p.total("sw")])
                            w_t = p.op("vector", lambda E, gbuf2=gbuf2: E.tensor_tensor(
                                out=gbuf2, in0=sfs, in1=gbuf2, op=ALU.mult),
                                wait=[ev_g, w_ls])
                            p.wait_only("sync", [w_t])
                            w_xa = p.dma("gpsimd", gbuf2,
                                         x_win[TO + mg * 128:TO + (mg + 1) * 128,
                                               1024 * Hh:1024 * (Hh + 1)],
                                         f"xa{gi}", wait=[w_t], accum=ALU.add)
                            w_od = p.dma("gpsimd", out_tm[mg * 128:(mg + 1) * 128,
                                                          1024 * Hh:1024 * (Hh + 1)],
                                         gbuf2, f"ow{gi}", wait=[w_xa])
                            gfs_prev[gi] = w_od

        if debug:
            dcp = SCR[:, 0:2048]
            if n_phases < 3:
                for j in range(NJ):
                    w_c = p.op("vector", lambda E, j=j: E.tensor_copy(
                        dcp, xn_fm[:, j * TW:(j + 1) * TW]), wait=[tr_done[-1], p.bar("scalar")])
                    w_d = p.dma("gpsimd", dbg["xn_fm"][j], dcp, "gd", wait=[w_c])
                    p.wait_only("vector", [w_d])
                for j in range(NJ):
                    w_c = p.op("vector", lambda E, j=j: E.tensor_copy(
                        dcp[:, 0:TO], xq_fm[:, j * TO:(j + 1) * TO]), wait=[trq_done[-1]])
                    w_d = p.dma("gpsimd", dbg["xq_fm"][j], dcp[:, 0:TO], "gd", wait=[w_c])
                    p.wait_only("vector", [w_d])
            if n_phases >= 2:
                w_c2 = p.dma("gpsimd", dbg["s_fm"], s_scr[:, :], "gd", wait=[s_dma[-1]])
            if n_phases == 3:
                p.dma("gpsimd", dbg["gwq"], gwq[0:256, :], "gd", wait=W_done)
            if n_phases >= 5:
                p.dma("gpsimd", dbg["dwq"], dwq[0:256, :], "gd", wait=W_done)
            if n_phases == 4:
                for c in range(NIC):
                    w_c = p.op("vector", lambda E, c=c: E.tensor_copy(
                        dcp[:, 0:512], hq(c)), wait=[hq_evs[-1]])
                    w_d = p.dma("gpsimd", dbg["hq"][c][:, 0:512], dcp[:, 0:512], "gd", wait=[w_c])
                    p.wait_only("vector", [w_d])

        fin = []
        for nm in ("ow0", "ow1", "gd"):
            if nm in p.sems:
                fin.append(p.total(nm))
        p.wait_only("gpsimd", fin)
        with nc.Block() as block:
            p.install(block)
    return nc




_CACHE = {}


def kernel(L0_attn_x, L0_attn_W_in, L0_attn_b_in, L0_attn_W_decay, L0_attn_b_decay,
           L0_attn_W_gate, L0_attn_b_gate, L0_attn_W_out, L0_attn_b_out,
           L0_attn_decay_bias, L0_ffn_gate_w, L0_ffn_up_w, L0_ffn_down_w):
    import ml_dtypes
    x = np.ascontiguousarray(np.asarray(L0_attn_x, dtype=np.float32))
    B = x.shape[0]
    ident = np.eye(128, dtype=ml_dtypes.bfloat16)
    bout_tile = np.ascontiguousarray(
        np.broadcast_to(np.asarray(L0_attn_b_out, np.float32), (128, D)))
    zeros_ctx = np.zeros((TO, D), np.float32)
    shared = {
        "ident": ident,
        "bout_tile": bout_tile,
        "w_in": np.ascontiguousarray(L0_attn_W_in, np.float32),
        "w_dec": np.ascontiguousarray(L0_attn_W_decay, np.float32),
        "w_gate": np.ascontiguousarray(L0_attn_W_gate, np.float32),
        "w_out": np.ascontiguousarray(L0_attn_W_out, np.float32),
        "b_in": np.ascontiguousarray(L0_attn_b_in, np.float32),
        "b_dec": np.ascontiguousarray(L0_attn_b_decay, np.float32),
        "b_gate": np.ascontiguousarray(L0_attn_b_gate, np.float32),
        "b_out": np.ascontiguousarray(L0_attn_b_out, np.float32),
        "dec_bias": np.ascontiguousarray(L0_attn_decay_bias, np.float32),
        "gw_nat": np.ascontiguousarray(L0_ffn_gate_w, np.float32),
        "uw_nat": np.ascontiguousarray(L0_ffn_up_w, np.float32),
        "dw_nat": np.ascontiguousarray(L0_ffn_down_w, np.float32),
    }
    in_maps = []
    for c in range(8):
        b, half = c // 2, c % 2
        if half == 0:
            x_win = np.concatenate([zeros_ctx, x[b, 0:TO]], axis=0)
        else:
            x_win = x[b]
        m = dict(shared)
        m["x_win"] = np.ascontiguousarray(x_win)
        m["bmask"] = np.full((128, 1), float(half), np.float32)
        in_maps.append(m)
    if "nc" not in _CACHE:
        _CACHE["nc"] = build()
    res = run_bass_kernel_spmd(_CACHE["nc"], in_maps, core_ids=list(range(8)))
    out = np.empty((B, 2048, D), np.float32)
    for c in range(8):
        b, half = c // 2, c % 2
        out[b, half * TO:(half + 1) * TO] = res.results[c]["out_tm"]
    return out


# version 06
# speedup vs baseline: 1.0481x; 1.0481x over previous
import numpy as np
import concourse.bass as bass
import concourse.mybir as mybir
from concourse.bass_utils import run_bass_kernel_spmd
from contextlib import ExitStack

F32 = mybir.dt.float32
BF16 = mybir.dt.bfloat16
F8E4 = mybir.dt.float8e4
ALU = mybir.AluOpType
AF = mybir.ActivationFunctionType
AX = mybir.AxisListType

D = 2048
TW = 2048
TO = 1024
II = 8192
NJ = D // 128
NTW = TW // 128
NTO = TO // 128
NIC = II // 128


def ts(i, n=128):
    return slice(i * n, (i + 1) * n)


class Sem:
    def __init__(self, nc, ctx, name):
        self.h = ctx.enter_context(nc.semaphore(name))
        self.n = 0


class Prog:
    ENGS = ("sync", "scalar", "vector", "tensor", "gpsimd")

    def __init__(self, nc, ctx):
        self.nc = nc
        self.ctx = ctx
        self.q = {e: [] for e in self.ENGS}
        self.sems = {e: Sem(nc, ctx, f"sem_{e}") for e in self.ENGS}

    def dsem(self, name):
        if name not in self.sems:
            self.sems[name] = Sem(self.nc, self.ctx, f"sem_{name}")
        return self.sems[name]

    def total(self, name):
        s = self.dsem(name)
        return (s, s.n)

    def dma(self, eng, out, in_, sem_name, wait=None, slow=False, accum=None):
        s = self.dsem(sem_name)
        s.n += 16
        tgt = s.n
        w = list(wait or [])

        def f(E, out=out, in_=in_, w=w, s=s, slow=slow, accum=accum):
            for (ws, wv) in w:
                E.wait_ge(ws.h, wv)
            if accum is not None:
                E.dma_start(out=out, in_=in_, accum_op=accum,
                            allow_slow_non_contiguous=slow).then_inc(s.h, 16)
            else:
                E.dma_start(out=out, in_=in_,
                            allow_slow_non_contiguous=slow).then_inc(s.h, 16)
        self.q[eng].append(f)
        return (s, tgt)

    def op(self, eng, fn, wait=None):
        s = self.sems[eng]
        s.n += 1
        w = list(wait or [])

        def f(E, fn=fn, w=w, s=s):
            for (ws, wv) in w:
                E.wait_ge(ws.h, wv)
            r = fn(E)
            if isinstance(r, list):
                r = r[-1]
            r.then_inc(s.h, 1)
        self.q[eng].append(f)
        return (s, tgt_of(s))

    def wait_only(self, eng, wait):
        w = list(wait)

        def f(E, w=w):
            for (ws, wv) in w:
                E.wait_ge(ws.h, wv)
        self.q[eng].append(f)

    def bar(self, eng):
        s = self.sems[eng]
        return (s, s.n)

    def install(self, block):
        for e in self.ENGS:
            thunks = self.q[e]

            def mk(thunks):
                def body(E):
                    for t in thunks:
                        t(E)
                return body
            getattr(block, e)(mk(thunks))


def tgt_of(s):
    return s.n


def build(n_phases=5, debug=False):
    nc = bass.Bass("TRN2", target_bir_lowering=False)
    x_win = nc.dram_tensor("x_win", [TW, D], F32, kind="ExternalInput").ap()
    bout_tile = nc.dram_tensor("bout_tile", [128, D], F32, kind="ExternalInput").ap()
    ident = nc.dram_tensor("ident", [128, 128], BF16, kind="ExternalInput").ap()
    w_in = nc.dram_tensor("w_in", [D, D], F32, kind="ExternalInput").ap()
    w_dec = nc.dram_tensor("w_dec", [D, D], F32, kind="ExternalInput").ap()
    w_gate = nc.dram_tensor("w_gate", [D, D], F32, kind="ExternalInput").ap()
    w_out = nc.dram_tensor("w_out", [D, D], F32, kind="ExternalInput").ap()
    b_in = nc.dram_tensor("b_in", [D], F32, kind="ExternalInput").ap()
    b_dec = nc.dram_tensor("b_dec", [D], F32, kind="ExternalInput").ap()
    b_gate = nc.dram_tensor("b_gate", [D], F32, kind="ExternalInput").ap()
    b_out = nc.dram_tensor("b_out", [D], F32, kind="ExternalInput").ap()
    dec_bias = nc.dram_tensor("dec_bias", [D], F32, kind="ExternalInput").ap()
    bmask = nc.dram_tensor("bmask", [128, 1], F32, kind="ExternalInput").ap()
    gw_nat = nc.dram_tensor("gw_nat", [II, D], F32, kind="ExternalInput").ap()
    uw_nat = nc.dram_tensor("uw_nat", [II, D], F32, kind="ExternalInput").ap()
    dw_nat = nc.dram_tensor("dw_nat", [D, II], F32, kind="ExternalInput").ap()
    out_tm = nc.dram_tensor("out_tm", [TO, D], F32, kind="ExternalOutput").ap()
    s_scr = nc.dram_tensor("s_scr", [TO, D], F32, kind="Internal").ap()
    gwq = nc.dram_tensor("gwq", [D, II], BF16, kind="Internal").ap()
    uwq = nc.dram_tensor("uwq", [D, II], BF16, kind="Internal").ap()
    dwq = nc.dram_tensor("dwq", [II, D], BF16, kind="Internal").ap()
    dbg = {}
    if debug:
        dbg["xn_fm"] = nc.dram_tensor("dbg_xn_fm", [NJ, 128, TW], F32, kind="ExternalOutput").ap()
        dbg["xq_fm"] = nc.dram_tensor("dbg_xq_fm", [NJ, 128, TO], F32, kind="ExternalOutput").ap()
        dbg["s_fm"] = nc.dram_tensor("dbg_s_fm", [TO, D], F32, kind="ExternalOutput").ap()
        dbg["hq"] = nc.dram_tensor("dbg_hq", [NIC, 128, TO], F32, kind="ExternalOutput").ap()
        dbg["gwq"] = nc.dram_tensor("dbg_gwq", [256, II], BF16, kind="ExternalOutput").ap()
        dbg["dwq"] = nc.dram_tensor("dbg_dwq", [256, D], BF16, kind="ExternalOutput").ap()

    with ExitStack() as ctx:
        S = lambda *a: ctx.enter_context(nc.sbuf_tensor(*a))
        PS = lambda *a: ctx.enter_context(nc.psum_tensor(*a))
        xn_fm = S("xn_fm", [128, NJ * TW], BF16)
        xq_fm = S("xq_fm", [128, NJ * TO], BF16)
        y_fm = S("y_fm", [128, NJ * TO], BF16)
        SCR = S("SCR", [128, 19840], F32)
        stat = S("stat", [128, 16], F32)
        bias_c = S("bias_c", [128, 96], F32)
        idt = S("idt", [128, 128], BF16)
        qsc = S("qsc", [128, 64], F32)
        qs2 = S("qs2", [128, 64], F32)
        qrr = S("qrr", [128, 64], F32)
        pA = PS("pA", [128, 2048], F32)
        pB = PS("pB", [128, 2048], F32)
        pBb = pB[:].bitcast(BF16)
        pTa = pBb[:, 0:1024]
        pTb = pBb[:, 1024:2048]

        hq = lambda c: xn_fm[:, c * 512:(c + 1) * 512]

        p = Prog(nc, ctx)
        w_id = p.dma("sync", idt[:], ident, "ld_id")

        xw = [SCR[:, 0:2048], SCR[:, 2048:4096]]
        xnbf = [SCR[:, 4096:5120].bitcast(BF16), SCR[:, 5120:6144].bitcast(BF16)]
        junk = SCR[:, 6144:7168].bitcast(BF16)
        q8 = SCR[:, 7168:7680].bitcast(F8E4)
        xqbf = [SCR[:, 7680:8704].bitcast(BF16), SCR[:, 8704:9728].bitcast(BF16)]

        tr_done = [((p.sems["scalar"]), 0)]
        trq_done = [((p.sems["scalar"]), 0)]
        for t in range(NTW):
            buf = xw[t % 2]
            nb = xnbf[t % 2]
            own = t >= NTW - NTO
            wait_reuse = [tr_done[-2]] if len(tr_done) >= 2 else []
            w_x = p.dma("sync", buf, x_win[ts(t), :], f"x{t % 2}", wait=wait_reuse)
            w_sum = p.op("vector", lambda E, buf=buf: E.tensor_reduce(
                out=stat[:, 0:1], in_=buf, axis=AX.X, op=ALU.add), wait=[w_x])
            w_nm = p.op("vector", lambda E: E.tensor_scalar(
                out=stat[:, 1:2], in0=stat[:, 0:1], scalar1=1.0 / D, scalar2=-1.0,
                op0=ALU.mult, op1=ALU.mult), wait=[w_sum])
            w_pm = p.op("vector", lambda E: E.tensor_scalar(
                out=stat[:, 2:3], in0=stat[:, 0:1], scalar1=1.0 / D, scalar2=None,
                op0=ALU.mult), wait=[w_sum])
            w_sq = p.op("scalar", lambda E, buf=buf: E.activation(
                out=junk, in_=buf, func=AF.Square, bias=stat[:, 1:2],
                accum_out=stat[:, 3:4]), wait=[w_nm])
            w_var = p.op("vector", lambda E: E.tensor_scalar(
                out=stat[:, 4:5], in0=stat[:, 3:4], scalar1=1.0 / D, scalar2=1e-5,
                op0=ALU.mult, op1=ALU.add), wait=[w_sq])
            w_sqr = p.op("scalar", lambda E: E.activation(
                out=stat[:, 5:6], in_=stat[:, 4:5], func=AF.Sqrt), wait=[w_var])
            w_rstd = p.op("vector", lambda E: E.reciprocal(
                out=stat[:, 6:7], in_=stat[:, 5:6]), wait=[w_sqr])
            if own:
                to = t - (NTW - NTO)
                xn32 = SCR[:, 9728:11776]
                w_xn32 = p.op("vector", lambda E, buf=buf, xn32=xn32: E.tensor_scalar(
                    out=xn32, in0=buf, scalar1=stat[:, 2:3], scalar2=stat[:, 6:7],
                    op0=ALU.subtract, op1=ALU.mult), wait=[w_rstd])
                w_xnbf = p.op("vector", lambda E, nb=nb, xn32=xn32: E.tensor_copy(
                    nb, xn32), wait=[w_xn32])
            else:
                w_xnbf = p.op("vector", lambda E, buf=buf, nb=nb: E.tensor_scalar(
                    out=nb, in0=buf, scalar1=stat[:, 2:3], scalar2=stat[:, 6:7],
                    op0=ALU.subtract, op1=ALU.mult), wait=[w_rstd])
            last = None
            for jg in range(2):
                sc_bar = p.bar("scalar")
                w_tr = p.op("tensor", lambda E, nb=nb, jg=jg: [
                    E.transpose(pTa[:, 128 * k:128 * (k + 1)], nb[:, ts(8 * jg + k)], idt[:])
                    for k in range(8)], wait=[w_xnbf, w_id, sc_bar])
                w_ev = p.op("scalar", lambda E, t=t, jg=jg: [
                    E.copy(xn_fm[:, (8 * jg + k) * TW + t * 128:(8 * jg + k) * TW + (t + 1) * 128],
                           pTa[:, 128 * k:128 * (k + 1)]) for k in range(8)], wait=[w_tr])
                last = w_ev
            tr_done.append(last)
            if own:
                to = t - (NTW - NTO)
                qb = xqbf[t % 2]
                a3 = xn32.rearrange("p (b c) -> p b c", c=32)
                w_red = p.op("vector", lambda E, a3=a3: E.tensor_reduce(
                    out=qsc[:], in_=a3, axis=AX.X, op=ALU.max,
                    apply_absolute_value=True), wait=[w_xn32])
                w_s2 = p.op("vector", lambda E: E.tensor_scalar(
                    out=qs2[:], in0=qsc[:], scalar1=1.0 / 224.0, scalar2=2e-30,
                    op0=ALU.mult, op1=ALU.max), wait=[w_red])
                w_r = p.op("vector", lambda E: E.reciprocal(out=qrr[:], in_=qs2[:]),
                           wait=[w_s2])
                rb = qrr[:].unsqueeze(2).broadcast_to([128, 64, 32])
                s2b = qs2[:].unsqueeze(2).broadcast_to([128, 64, 32])
                wait_reuse = [trq_done[-2]] if len(trq_done) >= 2 else []
                w_q = p.op("vector", lambda E, a3=a3, rb=rb: E.tensor_tensor(
                    out=q8.rearrange("p (b c) -> p b c", c=32), in0=a3, in1=rb,
                    op=ALU.mult), wait=[w_r] + wait_reuse)
                w_dq = p.op("vector", lambda E, qb=qb, s2b=s2b: E.tensor_tensor(
                    out=qb.rearrange("p (b c) -> p b c", c=32),
                    in0=q8.rearrange("p (b c) -> p b c", c=32), in1=s2b,
                    op=ALU.mult), wait=[w_q])
                last = None
                for jg in range(2):
                    sc_bar = p.bar("scalar")
                    w_tr = p.op("tensor", lambda E, qb=qb, jg=jg: [
                        E.transpose(pTa[:, 128 * k:128 * (k + 1)], qb[:, ts(8 * jg + k)], idt[:])
                        for k in range(8)], wait=[w_dq, sc_bar])
                    w_ev = p.op("scalar", lambda E, to=to, jg=jg: [
                        E.copy(xq_fm[:, (8 * jg + k) * TO + to * 128:(8 * jg + k) * TO + (to + 1) * 128],
                               pTa[:, 128 * k:128 * (k + 1)]) for k in range(8)], wait=[w_tr])
                    last = w_ev
                trq_done.append(last)

        A_done_sc = p.bar("scalar")
        A_done_ve = p.bar("vector")

        if n_phases >= 2:
            bi = lambda i: bias_c[:, 16 * i:16 * (i + 1)]
            p.dma("sync", bi(0), b_in.rearrange("(j p) -> p j", p=128), "ld_bias", slow=True)
            p.dma("sync", bi(1), b_dec.rearrange("(j p) -> p j", p=128), "ld_bias", slow=True)
            p.dma("sync", bi(3), b_gate.rearrange("(j p) -> p j", p=128), "ld_bias", slow=True)
            p.dma("sync", bias_c[:, 80:96], dec_bias.rearrange("(j p) -> p j", p=128), "ld_bias", slow=True)
            p.dma("sync", stat[:, 7:8], bmask, "ld_bias")
            w_b1 = w_b2 = w_b3 = w_b5 = w_bm = p.total("ld_bias")
            w_bd = p.op("vector", lambda E: E.tensor_tensor(
                out=bi(1), in0=bi(1), in1=bias_c[:, 80:96], op=ALU.add),
                wait=[w_b2, w_b5])
            w_bn = p.op("vector", lambda E: E.tensor_scalar(
                out=bi(2), in0=bi(1), scalar1=-1.0, scalar2=None, op0=ALU.mult),
                wait=[w_bd])

            panF = [SCR[:, 0:2048], SCR[:, 2048:4096]]
            abuf = SCR[:, 4096:6144]
            ibuf = SCR[:, 6144:8192]
            bbuf = SCR[:, 8192:10240]
            gbuf = SCR[:, 10240:11264]
            s_buf = SCR[:, 11264:12288]
            panW = [SCR[:, 11648 + 1024 * i:11648 + 1024 * (i + 1)].bitcast(BF16)
                    for i in range(6)]
            panO = [SCR[:, 17792 + 1024 * i:17792 + 1024 * (i + 1)].bitcast(BF16)
                    for i in range(2)]

            pan_ready = {}
            panw_prev = {}
            panf_prev = {}

            def prefetch(j):
                for wi, wten in enumerate((w_in, w_dec, w_gate)):
                    fs = panF[wi % 2]
                    dst = panW[2 * wi + (j % 2)]
                    wx = []
                    if (2 * wi + (j % 2)) in panw_prev:
                        wx.append(panw_prev[2 * wi + (j % 2)])
                    key = (wi % 2)
                    wfs = [panf_prev[key]] if key in panf_prev else []
                    src = wten.rearrange("(k p) e -> p k e", p=128)[:, :, ts(j)]
                    if j == 0 and wi == 0:
                        wx = wx + [A_done_sc, A_done_ve]
                    w_l = p.dma("sync", fs, src, f"pf{wi % 2}", wait=wx + wfs)
                    w_c = p.op("vector", lambda E, fs=fs, dst=dst: E.tensor_copy(dst, fs),
                               wait=[w_l])
                    panf_prev[key] = w_c
                    pan_ready[(wi, j)] = w_c

            prefetch(0)
            w_y_all = []
            for j in range(NJ):
                if j + 1 < NJ:
                    prefetch(j + 1)
                pin = panW[0 + (j % 2)]
                pdec = panW[2 + (j % 2)]
                pgate = panW[4 + (j % 2)]
                sc_bar = p.bar("scalar")
                ve_bar = p.bar("vector")
                w_mm_in = p.op("tensor", lambda E, pin=pin: [
                    E.matmul(pA[:, 512 * t4:512 * (t4 + 1)], pin[:, ts(k)],
                             xn_fm[:, k * TW + 512 * t4:k * TW + 512 * (t4 + 1)],
                             start=(k == 0), stop=(k == NJ - 1))
                    for k in range(NJ) for t4 in range(4)],
                    wait=[pan_ready[(0, j)], sc_bar, ve_bar])
                w_mm_dec = p.op("tensor", lambda E, pdec=pdec: [
                    E.matmul(pB[:, 512 * t4:512 * (t4 + 1)], pdec[:, ts(k)],
                             xn_fm[:, k * TW + 512 * t4:k * TW + 512 * (t4 + 1)],
                             start=(k == 0), stop=(k == NJ - 1))
                    for k in range(NJ) for t4 in range(4)],
                    wait=[pan_ready[(1, j)]])
                ev_in = p.op("scalar", lambda E, j=j: E.activation(
                    out=ibuf, in_=pA[:], func=AF.Silu, bias=bi(0)[:, j:j + 1]),
                    wait=[w_mm_in, w_b1])
                ev_a = p.op("scalar", lambda E, j=j: E.activation(
                    out=abuf, in_=pB[:], func=AF.Sigmoid, bias=bi(1)[:, j:j + 1]),
                    wait=[w_mm_dec, w_bd])
                ev_oma = p.op("scalar", lambda E, j=j: E.activation(
                    out=bbuf, in_=pB[:], func=AF.Sigmoid, bias=bi(2)[:, j:j + 1],
                    scale=-1.0), wait=[w_bn])
                w_mm_g = p.op("tensor", lambda E, pgate=pgate: [
                    E.matmul(pA[:, 512 * t4:512 * (t4 + 1)], pgate[:, ts(k)],
                             xn_fm[:, k * TW + TO + 512 * t4:k * TW + TO + 512 * (t4 + 1)],
                             start=(k == 0), stop=(k == NJ - 1))
                    for k in range(NJ) for t4 in range(2)],
                    wait=[pan_ready[(2, j)], ev_in])
                panw_prev[0 + (j % 2)] = ev_in
                panw_prev[2 + (j % 2)] = ev_a
                panw_prev[4 + (j % 2)] = w_mm_g
                w_b = p.op("vector", lambda E: E.tensor_tensor(
                    out=bbuf, in0=bbuf, in1=ibuf, op=ALU.mult),
                    wait=[ev_in, ev_oma, w_bm])
                w_bmk = p.op("vector", lambda E: E.tensor_scalar(
                    out=bbuf[:, 0:TO], in0=bbuf[:, 0:TO], scalar1=stat[:, 7:8],
                    scalar2=None, op0=ALU.mult), wait=[w_b])
                w_scan = p.op("vector", lambda E: E.tensor_tensor_scan(
                    out=ibuf, data0=abuf, data1=bbuf, initial=0.0,
                    op0=ALU.mult, op1=ALU.add), wait=[w_bmk, ev_a])
                ev_gate = p.op("scalar", lambda E, j=j: E.activation(
                    out=gbuf, in_=pA[:, 0:TO], func=AF.Sigmoid, bias=bi(3)[:, j:j + 1]),
                    wait=[w_mm_g, w_b3])
                w_y = p.op("vector", lambda E, j=j: E.tensor_tensor(
                    out=y_fm[:, j * TO:(j + 1) * TO], in0=gbuf, in1=ibuf[:, TO:TW],
                    op=ALU.mult), wait=[w_scan, ev_gate])
                w_y_all.append(w_y)

            te_bar_c1 = p.bar("tensor")
            WoutBF = lambda k: xn_fm[:, k * 2048:(k + 1) * 2048]
            bout_t = SCR[:, 4096:6144]
            s_buf2 = SCR[:, 6144:8192]
            ve_bar0 = p.bar("vector")
            w_bt = p.dma("sync", bout_t, bout_tile, "ld_bout", wait=[ve_bar0])
            wo_ready = []
            for k in range(NJ):
                fs = panF[k % 2]
                wfs = [panf_prev[k % 2]] if (k % 2) in panf_prev else []
                w_l = p.dma("sync", fs, w_out[ts(k), :], f"pf{k % 2}", wait=wfs)
                w_c = p.op("vector", lambda E, fs=fs, k=k: E.tensor_copy(
                    WoutBF(k), fs), wait=[w_l] + ([te_bar_c1, w_y_all[-1]] if k < 2 else []))
                panf_prev[k % 2] = w_c
                wo_ready.append(w_c)
            s_dma = []
            for m in range(8):
                ps = pA if m % 2 == 0 else pB
                ve_bar = p.bar("vector")
                sc_bar = p.bar("scalar")
                w_mm_o = p.op("tensor", lambda E, m=m, ps=ps: [
                    E.matmul(ps[:, 512 * t4:512 * (t4 + 1)],
                             y_fm[:, k * TO + m * 128:k * TO + (m + 1) * 128],
                             WoutBF(k)[:, 512 * t4:512 * (t4 + 1)],
                             start=(k == 0), stop=(k == NJ - 1))
                    for k in range(NJ) for t4 in range(4)],
                    wait=[wo_ready[-1], w_y_all[-1], ve_bar, sc_bar])
                w_s = p.op("vector", lambda E, ps=ps: E.tensor_tensor(
                    out=s_buf2, in0=ps[:], in1=bout_t, op=ALU.add),
                    wait=[w_mm_o, w_bt])
                w_sd = p.dma("gpsimd", s_scr[m * 128:(m + 1) * 128, :], s_buf2, "sw",
                             wait=[w_s])
                p.wait_only("vector", [w_sd])
                s_dma.append(w_sd)

        if n_phases >= 3:
            natS = [SCR[:, 0:4096], SCR[:, 4096:8192]]
            deqBB = [[SCR[:, 8192 + 4096 * s + 1024 * r:8192 + 4096 * s + 1024 * (r + 1)].bitcast(BF16)
                      for r in range(4)] for s in range(2)]
            wstg = [SCR[:, 16384 + 256 * i:16384 + 256 * (i + 1)].bitcast(BF16)
                    for i in range(8)]
            qscW = qsc[:]
            qs2W = qs2[:]
            qrW = qrr[:]
            q8W = SCR[:, 18432:18944].bitcast(F8E4)
            nat_prev = {}
            wstg_prev = {}
            deq_guard = {}
            blkcc_i = [0]

            def prep_weight(nat, out, R, C, tag):
                nch = C // 2048
                natv = nat.rearrange("(kk p) c -> p kk c", p=128)
                for blk in range(R // 512):
                    for cc in range(nch):
                        bci = blkcc_i[0]
                        blkcc_i[0] += 1
                        deqB = deqBB[0]
                        deq_ready = []
                        for r in range(4):
                            fs = natS[(r // 2) % 2][:, 2048 * (r % 2):2048 * (r % 2 + 1)]
                            if r % 2 == 0:
                                rp = r // 2
                                slot = rp % 2
                                wx = [nat_prev[slot]] if slot in nat_prev else []
                                if blk == 0 and cc == 0 and tag == "g" and n_phases >= 2:
                                    wx += [p.bar("scalar"), p.bar("vector")]
                                w_l = p.dma("sync", natS[slot],
                                            natv[:, 4 * blk + 2 * rp:4 * blk + 2 * rp + 2,
                                                 cc * 2048:(cc + 1) * 2048],
                                            f"pf{slot}", wait=wx)
                            a3 = fs.rearrange("p (b c) -> p b c", c=32)
                            w_red = p.op("vector", lambda E, a3=a3: E.tensor_reduce(
                                out=qscW, in_=a3, axis=AX.X, op=ALU.max,
                                apply_absolute_value=True), wait=[w_l])
                            w_s2 = p.op("vector", lambda E: E.tensor_scalar(
                                out=qs2W, in0=qscW, scalar1=1.0 / 224.0, scalar2=2e-30,
                                op0=ALU.mult, op1=ALU.max), wait=[w_red])
                            w_r = p.op("vector", lambda E: E.reciprocal(
                                out=qrW, in_=qs2W), wait=[w_s2])
                            rb = qrW.unsqueeze(2).broadcast_to([128, 64, 32])
                            s2b = qs2W.unsqueeze(2).broadcast_to([128, 64, 32])
                            wdq = [deq_guard[0]] if (r == 0 and 0 in deq_guard) else []
                            w_q = p.op("vector", lambda E, a3=a3, rb=rb: E.tensor_tensor(
                                out=q8W.rearrange("p (b c) -> p b c", c=32), in0=a3,
                                in1=rb, op=ALU.mult), wait=[w_r] + wdq)
                            if r % 2 == 1:
                                nat_prev[(r // 2) % 2] = w_q
                            w_dq = p.op("vector", lambda E, r=r, s2b=s2b: E.tensor_tensor(
                                out=deqB[r].rearrange("p (b c) -> p b c", c=32),
                                in0=q8W.rearrange("p (b c) -> p b c", c=32), in1=s2b,
                                op=ALU.mult), wait=[w_q])
                            deq_ready.append(w_dq)
                        for hc in range(16):
                            pt = pTa if hc % 2 == 0 else pTb
                            sc_bar = p.bar("scalar")
                            w_tr = p.op("tensor", lambda E, hc=hc, pt=pt, deqB=deqB: [
                                E.transpose(pt[:, 128 * r:128 * (r + 1)],
                                            deqB[r][:, ts(hc)], idt[:])
                                for r in range(4)], wait=deq_ready + [sc_bar])
                            wg = [wstg_prev[hc % 8]] if (hc % 8) in wstg_prev else []
                            w_ev = p.op("scalar", lambda E, hc=hc, pt=pt: E.copy(
                                wstg[hc % 8][:], pt[:, 0:512]), wait=[w_tr] + wg)
                            w_wr = p.dma("gpsimd", out[cc * 2048 + hc * 128:cc * 2048 + (hc + 1) * 128,
                                                      blk * 512:(blk + 1) * 512],
                                         wstg[hc % 8][:], f"ws{hc % 8}", wait=[w_ev])
                            wstg_prev[hc % 8] = w_wr
                        deq_guard[0] = w_tr
            prep_weight(gw_nat, gwq, II, D, "g")
            prep_weight(uw_nat, uwq, II, D, "u")
            prep_weight(dw_nat, dwq, D, II, "d")
            W_done = [p.total(f"ws{i}") for i in range(8)]

        if n_phases >= 4:
            panE = [y_fm[:, 0:8192], y_fm[:, 8192:16384]]
            panE2 = [SCR[:, 1600 + 4096 * i:1600 + 4096 * (i + 1)].bitcast(BF16)
                     for i in range(2)]
            gsil = SCR[:, 0:512]
            hbuf = SCR[:, 512:1024]
            q8e = SCR[:, 1024:1152].bitcast(F8E4)
            hqb = SCR[:, 1152:1408].bitcast(BF16)
            qscE = SCR[:, 1536:1552]
            qs2E = SCR[:, 1552:1568]
            qrE = SCR[:, 1568:1584]
            panE_prev = {}
            dpanF = [SCR[:, 9792 + 2048 * i:9792 + 2048 * (i + 1)].bitcast(BF16)
                     for i in range(2)]
            gfs2 = [SCR[:, 13888:14912], SCR[:, 14912:15936]]
            sfs = SCR[:, 15936:16960]
            dpan_prev = {}
            gfs_prev = {}

            def pan_pair(i5):
                return (panE[0], panE[1]) if i5 % 2 == 0 else (panE2[0], panE2[1])

            def prefetch_e(th, i5):
                gp, up = pan_pair(i5)
                key = i5 % 2
                wx = [panE_prev[key]] if key in panE_prev else []
                if th == 0 and i5 < 2:
                    wx = wx + [p.bar("scalar"), p.bar("vector")] + W_done
                w_g = p.dma("sync", gp, gwq.rearrange("(k p) i -> p k i", p=128)[:, :, 512 * i5:512 * (i5 + 1)],
                            f"eg{i5 % 2}", wait=wx)
                w_u = p.dma("sync", up, uwq.rearrange("(k p) i -> p k i", p=128)[:, :, 512 * i5:512 * (i5 + 1)],
                            f"eu{i5 % 2}", wait=[])
                return (w_g, w_u)

            hq_evs = []
            hq_tr_prev = [None]
            for th in range(2):
                pr = prefetch_e(th, 0)
                e_ready = {0: pr}
                mm_prev = {}
                for i5 in range(16):
                    if i5 + 1 < 16:
                        e_ready[i5 + 1] = prefetch_e(th, i5 + 1)
                    gp, up = pan_pair(i5)
                    for m in range(4):
                        half = (m % 2)
                        psg = pA[:, 1024 * half:1024 * half + 512]
                        psu = pA[:, 1024 * half + 512:1024 * half + 1024]
                        sc_bar = p.bar("scalar")
                        ve_bar = p.bar("vector")
                        w_mm = p.op("tensor", lambda E, gp=gp, up=up, th=th, m=m, psg=psg, psu=psu: [
                            E.matmul(pg_ps, lhsT, rhs, start=(k == 0), stop=(k == NJ - 1))
                            for k in range(NJ)
                            for (pg_ps, lhsT, rhs) in (
                                (psg, xq_fm[:, k * TO + th * 512 + m * 128:k * TO + th * 512 + (m + 1) * 128],
                                 gp[:, 512 * k:512 * (k + 1)]),
                                (psu, xq_fm[:, k * TO + th * 512 + m * 128:k * TO + th * 512 + (m + 1) * 128],
                                 up[:, 512 * k:512 * (k + 1)]),
                            )],
                            wait=list(e_ready[i5]) + [sc_bar, ve_bar])
                        mm_prev[half] = w_mm
                        ev_sil = p.op("scalar", lambda E, psg=psg: E.activation(
                            out=gsil, in_=psg, func=AF.Silu), wait=[w_mm])
                        w_h = p.op("vector", lambda E, psu=psu: E.tensor_tensor(
                            out=hbuf, in0=gsil, in1=psu, op=ALU.mult), wait=[ev_sil])
                        a3 = hbuf.rearrange("p (b c) -> p b c", c=32)
                        w_red = p.op("vector", lambda E, a3=a3: E.tensor_reduce(
                            out=qscE, in_=a3, axis=AX.X, op=ALU.max,
                            apply_absolute_value=True), wait=[w_h])
                        w_s2 = p.op("vector", lambda E: E.tensor_scalar(
                            out=qs2E, in0=qscE, scalar1=1.0 / 224.0, scalar2=2e-30,
                            op0=ALU.mult, op1=ALU.max), wait=[w_red])
                        w_r = p.op("vector", lambda E: E.reciprocal(
                            out=qrE, in_=qs2E), wait=[w_s2])
                        rb = qrE.unsqueeze(2).broadcast_to([128, 16, 32])
                        s2b = qs2E.unsqueeze(2).broadcast_to([128, 16, 32])
                        w_q = p.op("vector", lambda E, a3=a3, rb=rb: E.tensor_tensor(
                            out=q8e.rearrange("p (b c) -> p b c", c=32), in0=a3,
                            in1=rb, op=ALU.mult),
                            wait=[w_r] + ([hq_tr_prev[0]] if hq_tr_prev[0] else []))
                        w_dq = p.op("vector", lambda E, s2b=s2b: E.tensor_tensor(
                            out=hqb.rearrange("p (b c) -> p b c", c=32),
                            in0=q8e.rearrange("p (b c) -> p b c", c=32), in1=s2b,
                            op=ALU.mult), wait=[w_q])
                        pt = pTa if (i5 * 4 + m) % 2 == 0 else pTb
                        sc_bar2 = p.bar("scalar")
                        w_tr = p.op("tensor", lambda E, pt=pt: [
                            E.transpose(pt[:, 128 * t:128 * (t + 1)], hqb[:, ts(t)], idt[:])
                            for t in range(4)], wait=[w_dq, sc_bar2])
                        w_ev = p.op("scalar", lambda E, i5=i5, m=m, th=th, pt=pt: [
                            E.copy(hq(i5 * 4 + t)[:, m * 128:(m + 1) * 128],
                                   pt[:, 128 * t:128 * (t + 1)]) for t in range(4)],
                            wait=[w_tr])
                        hq_evs.append(w_ev)
                    panE_prev[i5 % 2] = w_ev
                E_done = [p.bar("scalar"), p.bar("vector")]

                if n_phases >= 5:
                    dwqv = dwq.rearrange("(kk p) h -> p kk h", p=128)
                    for Hh in range(2):
                        mm_last = None
                        for kb in range(16):
                            fs = dpanF[kb % 2]
                            wx = [dpan_prev[kb % 2]] if (kb % 2) in dpan_prev else []
                            if th == 0 and Hh == 0 and kb < 2:
                                wx = wx + E_done + [p.total("sw")]
                            w_l = p.dma("sync", fs,
                                        dwqv[:, 4 * kb:4 * kb + 4,
                                             1024 * Hh:1024 * (Hh + 1)],
                                        f"dp{kb % 2}", wait=wx)
                            bars = ([p.bar("vector"), p.bar("scalar")] if kb == 0 else [])
                            w_mm = p.op("tensor", lambda E, fs=fs, kb=kb, th=th: [
                                E.matmul((pA if m < 2 else pB)[:, 1024 * (m % 2) + 512 * t2:
                                                               1024 * (m % 2) + 512 * (t2 + 1)],
                                         hq(4 * kb + ks)[:, m * 128:(m + 1) * 128],
                                         fs[:, 1024 * ks + 512 * t2:1024 * ks + 512 * (t2 + 1)],
                                         start=(kb == 0 and ks == 0), stop=(kb == 15 and ks == 3))
                                for ks in range(4) for m in range(4) for t2 in range(2)],
                                wait=[w_l, hq_evs[-1]] + bars)
                            dpan_prev[kb % 2] = w_mm
                            mm_last = w_mm
                        for m in range(4):
                            mg = th * 4 + m
                            gi = m % 2
                            pset = (pA if m < 2 else pB)[:, 1024 * (m % 2):1024 * (m % 2 + 1)]
                            gbuf2 = gfs2[gi]
                            wg_re = [gfs_prev[gi]] if gi in gfs_prev else []
                            ev_g = p.op("scalar", lambda E, pset=pset, gbuf2=gbuf2: E.activation(
                                out=gbuf2, in_=pset, func=AF.Silu),
                                wait=[mm_last] + wg_re)
                            w_ls = p.dma("sync", sfs,
                                         s_scr[mg * 128:(mg + 1) * 128,
                                               1024 * Hh:1024 * (Hh + 1)],
                                         "fs", wait=[p.total("sw")])
                            w_t = p.op("vector", lambda E, gbuf2=gbuf2: E.tensor_tensor(
                                out=gbuf2, in0=sfs, in1=gbuf2, op=ALU.mult),
                                wait=[ev_g, w_ls])
                            p.wait_only("sync", [w_t])
                            w_xa = p.dma("gpsimd", gbuf2,
                                         x_win[TO + mg * 128:TO + (mg + 1) * 128,
                                               1024 * Hh:1024 * (Hh + 1)],
                                         f"xa{gi}", wait=[w_t], accum=ALU.add)
                            w_od = p.dma("gpsimd", out_tm[mg * 128:(mg + 1) * 128,
                                                          1024 * Hh:1024 * (Hh + 1)],
                                         gbuf2, f"ow{gi}", wait=[w_xa])
                            gfs_prev[gi] = w_od

        if debug:
            dcp = SCR[:, 0:2048]
            if n_phases < 3:
                for j in range(NJ):
                    w_c = p.op("vector", lambda E, j=j: E.tensor_copy(
                        dcp, xn_fm[:, j * TW:(j + 1) * TW]), wait=[tr_done[-1], p.bar("scalar")])
                    w_d = p.dma("gpsimd", dbg["xn_fm"][j], dcp, "gd", wait=[w_c])
                    p.wait_only("vector", [w_d])
                for j in range(NJ):
                    w_c = p.op("vector", lambda E, j=j: E.tensor_copy(
                        dcp[:, 0:TO], xq_fm[:, j * TO:(j + 1) * TO]), wait=[trq_done[-1]])
                    w_d = p.dma("gpsimd", dbg["xq_fm"][j], dcp[:, 0:TO], "gd", wait=[w_c])
                    p.wait_only("vector", [w_d])
            if n_phases >= 2:
                w_c2 = p.dma("gpsimd", dbg["s_fm"], s_scr[:, :], "gd", wait=[s_dma[-1]])
            if n_phases == 3:
                p.dma("gpsimd", dbg["gwq"], gwq[0:256, :], "gd", wait=W_done)
            if n_phases >= 5:
                p.dma("gpsimd", dbg["dwq"], dwq[0:256, :], "gd", wait=W_done)
            if n_phases == 4:
                for c in range(NIC):
                    w_c = p.op("vector", lambda E, c=c: E.tensor_copy(
                        dcp[:, 0:512], hq(c)), wait=[hq_evs[-1]])
                    w_d = p.dma("gpsimd", dbg["hq"][c][:, 0:512], dcp[:, 0:512], "gd", wait=[w_c])
                    p.wait_only("vector", [w_d])

        fin = []
        for nm in ("ow0", "ow1", "gd"):
            if nm in p.sems:
                fin.append(p.total(nm))
        p.wait_only("gpsimd", fin)
        with nc.Block() as block:
            p.install(block)
    return nc




_CACHE = {}


def kernel(L0_attn_x, L0_attn_W_in, L0_attn_b_in, L0_attn_W_decay, L0_attn_b_decay,
           L0_attn_W_gate, L0_attn_b_gate, L0_attn_W_out, L0_attn_b_out,
           L0_attn_decay_bias, L0_ffn_gate_w, L0_ffn_up_w, L0_ffn_down_w):
    import ml_dtypes
    x = np.ascontiguousarray(np.asarray(L0_attn_x, dtype=np.float32))
    B = x.shape[0]
    ident = np.eye(128, dtype=ml_dtypes.bfloat16)
    bout_tile = np.ascontiguousarray(
        np.broadcast_to(np.asarray(L0_attn_b_out, np.float32), (128, D)))
    zeros_ctx = np.zeros((TO, D), np.float32)
    shared = {
        "ident": ident,
        "bout_tile": bout_tile,
        "w_in": np.ascontiguousarray(L0_attn_W_in, np.float32),
        "w_dec": np.ascontiguousarray(L0_attn_W_decay, np.float32),
        "w_gate": np.ascontiguousarray(L0_attn_W_gate, np.float32),
        "w_out": np.ascontiguousarray(L0_attn_W_out, np.float32),
        "b_in": np.ascontiguousarray(L0_attn_b_in, np.float32),
        "b_dec": np.ascontiguousarray(L0_attn_b_decay, np.float32),
        "b_gate": np.ascontiguousarray(L0_attn_b_gate, np.float32),
        "b_out": np.ascontiguousarray(L0_attn_b_out, np.float32),
        "dec_bias": np.ascontiguousarray(L0_attn_decay_bias, np.float32),
        "gw_nat": np.ascontiguousarray(L0_ffn_gate_w, np.float32),
        "uw_nat": np.ascontiguousarray(L0_ffn_up_w, np.float32),
        "dw_nat": np.ascontiguousarray(L0_ffn_down_w, np.float32),
    }
    in_maps = []
    for c in range(8):
        b, half = c // 2, c % 2
        if half == 0:
            x_win = np.concatenate([zeros_ctx, x[b, 0:TO]], axis=0)
        else:
            x_win = x[b]
        m = dict(shared)
        m["x_win"] = np.ascontiguousarray(x_win)
        m["bmask"] = np.full((128, 1), float(half), np.float32)
        in_maps.append(m)
    if "nc" not in _CACHE:
        _CACHE["nc"] = build()
    res = run_bass_kernel_spmd(_CACHE["nc"], in_maps, core_ids=list(range(8)))
    out = np.empty((B, 2048, D), np.float32)
    for c in range(8):
        b, half = c // 2, c % 2
        out[b, half * TO:(half + 1) * TO] = res.results[c]["out_tm"]
    return out


# version 07
# speedup vs baseline: 1.1247x; 1.0731x over previous
import numpy as np
import concourse.bass as bass
import concourse.mybir as mybir
from concourse.bass_utils import run_bass_kernel_spmd
from contextlib import ExitStack

F32 = mybir.dt.float32
BF16 = mybir.dt.bfloat16
F8E4 = mybir.dt.float8e4
ALU = mybir.AluOpType
AF = mybir.ActivationFunctionType
AX = mybir.AxisListType

D = 2048
TW = 2048
TO = 1024
II = 8192
NJ = D // 128
NTW = TW // 128
NTO = TO // 128
NIC = II // 128


def ts(i, n=128):
    return slice(i * n, (i + 1) * n)


class Sem:
    def __init__(self, nc, ctx, name):
        self.h = ctx.enter_context(nc.semaphore(name))
        self.n = 0


class Prog:
    ENGS = ("sync", "scalar", "vector", "tensor", "gpsimd")

    def __init__(self, nc, ctx):
        self.nc = nc
        self.ctx = ctx
        self.q = {e: [] for e in self.ENGS}
        self.sems = {e: Sem(nc, ctx, f"sem_{e}") for e in self.ENGS}

    def dsem(self, name):
        if name not in self.sems:
            self.sems[name] = Sem(self.nc, self.ctx, f"sem_{name}")
        return self.sems[name]

    def total(self, name):
        s = self.dsem(name)
        return (s, s.n)

    def dma(self, eng, out, in_, sem_name, wait=None, slow=False, accum=None):
        s = self.dsem(sem_name)
        s.n += 16
        tgt = s.n
        w = list(wait or [])

        def f(E, out=out, in_=in_, w=w, s=s, slow=slow, accum=accum):
            for (ws, wv) in w:
                E.wait_ge(ws.h, wv)
            if accum is not None:
                E.dma_start(out=out, in_=in_, accum_op=accum,
                            allow_slow_non_contiguous=slow).then_inc(s.h, 16)
            else:
                E.dma_start(out=out, in_=in_,
                            allow_slow_non_contiguous=slow).then_inc(s.h, 16)
        self.q[eng].append(f)
        return (s, tgt)

    def op(self, eng, fn, wait=None):
        s = self.sems[eng]
        s.n += 1
        w = list(wait or [])

        def f(E, fn=fn, w=w, s=s):
            for (ws, wv) in w:
                E.wait_ge(ws.h, wv)
            r = fn(E)
            if isinstance(r, list):
                r = r[-1]
            r.then_inc(s.h, 1)
        self.q[eng].append(f)
        return (s, tgt_of(s))

    def wait_only(self, eng, wait):
        w = list(wait)

        def f(E, w=w):
            for (ws, wv) in w:
                E.wait_ge(ws.h, wv)
        self.q[eng].append(f)

    def bar(self, eng):
        s = self.sems[eng]
        return (s, s.n)

    def install(self, block):
        for e in self.ENGS:
            thunks = self.q[e]

            def mk(thunks):
                def body(E):
                    for t in thunks:
                        t(E)
                return body
            getattr(block, e)(mk(thunks))


def tgt_of(s):
    return s.n


def build(n_phases=5, debug=False):
    nc = bass.Bass("TRN2", target_bir_lowering=False)
    x_win = nc.dram_tensor("x_win", [TW, D], F32, kind="ExternalInput").ap()
    bout_tile = nc.dram_tensor("bout_tile", [128, D], F32, kind="ExternalInput").ap()
    ident = nc.dram_tensor("ident", [128, 128], BF16, kind="ExternalInput").ap()
    w_in = nc.dram_tensor("w_in", [D, D], F32, kind="ExternalInput").ap()
    w_dec = nc.dram_tensor("w_dec", [D, D], F32, kind="ExternalInput").ap()
    w_gate = nc.dram_tensor("w_gate", [D, D], F32, kind="ExternalInput").ap()
    w_out = nc.dram_tensor("w_out", [D, D], F32, kind="ExternalInput").ap()
    b_in = nc.dram_tensor("b_in", [D], F32, kind="ExternalInput").ap()
    b_dec = nc.dram_tensor("b_dec", [D], F32, kind="ExternalInput").ap()
    b_gate = nc.dram_tensor("b_gate", [D], F32, kind="ExternalInput").ap()
    b_out = nc.dram_tensor("b_out", [D], F32, kind="ExternalInput").ap()
    dec_bias = nc.dram_tensor("dec_bias", [D], F32, kind="ExternalInput").ap()
    bmask = nc.dram_tensor("bmask", [128, 1], F32, kind="ExternalInput").ap()
    gw_nat = nc.dram_tensor("gw_nat", [II, D], F32, kind="ExternalInput").ap()
    uw_nat = nc.dram_tensor("uw_nat", [II, D], F32, kind="ExternalInput").ap()
    dw_nat = nc.dram_tensor("dw_nat", [D, II], F32, kind="ExternalInput").ap()
    out_tm = nc.dram_tensor("out_tm", [TO, D], F32, kind="ExternalOutput").ap()
    s_scr = nc.dram_tensor("s_scr", [TO, D], F32, kind="Internal").ap()
    gwq = nc.dram_tensor("gwq", [D, II], BF16, kind="Internal").ap()
    uwq = nc.dram_tensor("uwq", [D, II], BF16, kind="Internal").ap()
    dwq = nc.dram_tensor("dwq", [II, D], BF16, kind="Internal").ap()
    dbg = {}
    if debug:
        dbg["xn_fm"] = nc.dram_tensor("dbg_xn_fm", [NJ, 128, TW], F32, kind="ExternalOutput").ap()
        dbg["xq_fm"] = nc.dram_tensor("dbg_xq_fm", [NJ, 128, TO], F32, kind="ExternalOutput").ap()
        dbg["s_fm"] = nc.dram_tensor("dbg_s_fm", [TO, D], F32, kind="ExternalOutput").ap()
        dbg["hq"] = nc.dram_tensor("dbg_hq", [NIC, 128, TO], F32, kind="ExternalOutput").ap()
        dbg["gwq"] = nc.dram_tensor("dbg_gwq", [256, II], BF16, kind="ExternalOutput").ap()
        dbg["dwq"] = nc.dram_tensor("dbg_dwq", [256, D], BF16, kind="ExternalOutput").ap()

    with ExitStack() as ctx:
        S = lambda *a: ctx.enter_context(nc.sbuf_tensor(*a))
        PS = lambda *a: ctx.enter_context(nc.psum_tensor(*a))
        xn_fm = S("xn_fm", [128, NJ * TW], BF16)
        xq_fm = S("xq_fm", [128, NJ * TO], BF16)
        y_fm = S("y_fm", [128, NJ * TO], BF16)
        SCR = S("SCR", [128, 19840], F32)
        stat = S("stat", [128, 16], F32)
        bias_c = S("bias_c", [128, 96], F32)
        idt = S("idt", [128, 128], BF16)
        qsc = S("qsc", [128, 64], F32)
        qs2 = S("qs2", [128, 64], F32)
        qrr = S("qrr", [128, 64], F32)
        pA = PS("pA", [128, 2048], F32)
        pB = PS("pB", [128, 2048], F32)
        pBb = pB[:].bitcast(BF16)
        pTa = pBb[:, 0:1024]
        pTb = pBb[:, 1024:2048]

        hq = lambda c: xn_fm[:, c * 512:(c + 1) * 512]

        p = Prog(nc, ctx)
        w_id = p.dma("sync", idt[:], ident, "ld_id")

        xw = [SCR[:, 0:2048], SCR[:, 2048:4096]]
        xnbf = [SCR[:, 4096:5120].bitcast(BF16), SCR[:, 5120:6144].bitcast(BF16)]
        junk = SCR[:, 6144:7168].bitcast(BF16)
        q8 = SCR[:, 7168:7680].bitcast(F8E4)
        xqbf = [SCR[:, 7680:8704].bitcast(BF16), SCR[:, 8704:9728].bitcast(BF16)]

        tr_done = [((p.sems["scalar"]), 0)]
        trq_done = [((p.sems["scalar"]), 0)]
        for t in range(NTW):
            buf = xw[t % 2]
            nb = xnbf[t % 2]
            own = t >= NTW - NTO
            wait_reuse = [tr_done[-2]] if len(tr_done) >= 2 else []
            w_x = p.dma("sync", buf, x_win[ts(t), :], f"x{t % 2}", wait=wait_reuse)
            w_sum = p.op("vector", lambda E, buf=buf: E.tensor_reduce(
                out=stat[:, 0:1], in_=buf, axis=AX.X, op=ALU.add), wait=[w_x])
            w_nm = p.op("vector", lambda E: E.tensor_scalar(
                out=stat[:, 1:2], in0=stat[:, 0:1], scalar1=1.0 / D, scalar2=-1.0,
                op0=ALU.mult, op1=ALU.mult), wait=[w_sum])
            w_pm = p.op("vector", lambda E: E.tensor_scalar(
                out=stat[:, 2:3], in0=stat[:, 0:1], scalar1=1.0 / D, scalar2=None,
                op0=ALU.mult), wait=[w_sum])
            w_sq = p.op("scalar", lambda E, buf=buf: E.activation(
                out=junk, in_=buf, func=AF.Square, bias=stat[:, 1:2],
                accum_out=stat[:, 3:4]), wait=[w_nm])
            w_var = p.op("vector", lambda E: E.tensor_scalar(
                out=stat[:, 4:5], in0=stat[:, 3:4], scalar1=1.0 / D, scalar2=1e-5,
                op0=ALU.mult, op1=ALU.add), wait=[w_sq])
            w_sqr = p.op("scalar", lambda E: E.activation(
                out=stat[:, 5:6], in_=stat[:, 4:5], func=AF.Sqrt), wait=[w_var])
            w_rstd = p.op("vector", lambda E: E.reciprocal(
                out=stat[:, 6:7], in_=stat[:, 5:6]), wait=[w_sqr])
            if own:
                to = t - (NTW - NTO)
                xn32 = SCR[:, 9728:11776]
                w_xn32 = p.op("vector", lambda E, buf=buf, xn32=xn32: E.tensor_scalar(
                    out=xn32, in0=buf, scalar1=stat[:, 2:3], scalar2=stat[:, 6:7],
                    op0=ALU.subtract, op1=ALU.mult), wait=[w_rstd])
                w_xnbf = p.op("vector", lambda E, nb=nb, xn32=xn32: E.tensor_copy(
                    nb, xn32), wait=[w_xn32])
            else:
                w_xnbf = p.op("vector", lambda E, buf=buf, nb=nb: E.tensor_scalar(
                    out=nb, in0=buf, scalar1=stat[:, 2:3], scalar2=stat[:, 6:7],
                    op0=ALU.subtract, op1=ALU.mult), wait=[w_rstd])
            last = None
            for jg in range(2):
                sc_bar = p.bar("scalar")
                w_tr = p.op("tensor", lambda E, nb=nb, jg=jg: [
                    E.transpose(pTa[:, 128 * k:128 * (k + 1)], nb[:, ts(8 * jg + k)], idt[:])
                    for k in range(8)], wait=[w_xnbf, w_id, sc_bar])
                w_ev = p.op("scalar", lambda E, t=t, jg=jg: [
                    E.copy(xn_fm[:, (8 * jg + k) * TW + t * 128:(8 * jg + k) * TW + (t + 1) * 128],
                           pTa[:, 128 * k:128 * (k + 1)]) for k in range(8)], wait=[w_tr])
                last = w_ev
            tr_done.append(last)
            if own:
                to = t - (NTW - NTO)
                qb = xqbf[t % 2]
                a3 = xn32.rearrange("p (b c) -> p b c", c=32)
                w_red = p.op("vector", lambda E, a3=a3: E.tensor_reduce(
                    out=qsc[:], in_=a3, axis=AX.X, op=ALU.max,
                    apply_absolute_value=True), wait=[w_xn32])
                w_s2 = p.op("vector", lambda E: E.tensor_scalar(
                    out=qs2[:], in0=qsc[:], scalar1=1.0 / 224.0, scalar2=2e-30,
                    op0=ALU.mult, op1=ALU.max), wait=[w_red])
                w_r = p.op("vector", lambda E: E.reciprocal(out=qrr[:], in_=qs2[:]),
                           wait=[w_s2])
                rb = qrr[:].unsqueeze(2).broadcast_to([128, 64, 32])
                s2b = qs2[:].unsqueeze(2).broadcast_to([128, 64, 32])
                wait_reuse = [trq_done[-2]] if len(trq_done) >= 2 else []
                w_q = p.op("vector", lambda E, a3=a3, rb=rb: E.tensor_tensor(
                    out=q8.rearrange("p (b c) -> p b c", c=32), in0=a3, in1=rb,
                    op=ALU.mult), wait=[w_r] + wait_reuse)
                w_dq = p.op("vector", lambda E, qb=qb, s2b=s2b: E.tensor_tensor(
                    out=qb.rearrange("p (b c) -> p b c", c=32),
                    in0=q8.rearrange("p (b c) -> p b c", c=32), in1=s2b,
                    op=ALU.mult), wait=[w_q])
                last = None
                for jg in range(2):
                    sc_bar = p.bar("scalar")
                    w_tr = p.op("tensor", lambda E, qb=qb, jg=jg: [
                        E.transpose(pTa[:, 128 * k:128 * (k + 1)], qb[:, ts(8 * jg + k)], idt[:])
                        for k in range(8)], wait=[w_dq, sc_bar])
                    w_ev = p.op("scalar", lambda E, to=to, jg=jg: [
                        E.copy(xq_fm[:, (8 * jg + k) * TO + to * 128:(8 * jg + k) * TO + (to + 1) * 128],
                               pTa[:, 128 * k:128 * (k + 1)]) for k in range(8)], wait=[w_tr])
                    last = w_ev
                trq_done.append(last)

        A_done_sc = p.bar("scalar")
        A_done_ve = p.bar("vector")

        if n_phases >= 2:
            bi = lambda i: bias_c[:, 16 * i:16 * (i + 1)]
            p.dma("sync", bi(0), b_in.rearrange("(j p) -> p j", p=128), "ld_bias", slow=True)
            p.dma("sync", bi(1), b_dec.rearrange("(j p) -> p j", p=128), "ld_bias", slow=True)
            p.dma("sync", bi(3), b_gate.rearrange("(j p) -> p j", p=128), "ld_bias", slow=True)
            p.dma("sync", bias_c[:, 80:96], dec_bias.rearrange("(j p) -> p j", p=128), "ld_bias", slow=True)
            p.dma("sync", stat[:, 7:8], bmask, "ld_bias")
            w_b1 = w_b2 = w_b3 = w_b5 = w_bm = p.total("ld_bias")
            w_bd = p.op("vector", lambda E: E.tensor_tensor(
                out=bi(1), in0=bi(1), in1=bias_c[:, 80:96], op=ALU.add),
                wait=[w_b2, w_b5])
            w_bn = p.op("vector", lambda E: E.tensor_scalar(
                out=bi(2), in0=bi(1), scalar1=-1.0, scalar2=None, op0=ALU.mult),
                wait=[w_bd])

            panF = [SCR[:, 0:2048], SCR[:, 2048:4096]]
            abuf = SCR[:, 4096:6144]
            ibuf = SCR[:, 6144:8192]
            bbuf = SCR[:, 8192:10240]
            gbuf = SCR[:, 10240:11264]
            s_buf = SCR[:, 11264:12288]
            panW = [SCR[:, 11648 + 1024 * i:11648 + 1024 * (i + 1)].bitcast(BF16)
                    for i in range(6)]
            panO = [SCR[:, 17792 + 1024 * i:17792 + 1024 * (i + 1)].bitcast(BF16)
                    for i in range(2)]

            pan_ready = {}
            panw_prev = {}
            panf_prev = {}

            def prefetch(j):
                for wi, wten in enumerate((w_in, w_dec, w_gate)):
                    fs = panF[wi % 2]
                    dst = panW[2 * wi + (j % 2)]
                    wx = []
                    if (2 * wi + (j % 2)) in panw_prev:
                        wx.append(panw_prev[2 * wi + (j % 2)])
                    key = (wi % 2)
                    wfs = [panf_prev[key]] if key in panf_prev else []
                    src = wten.rearrange("(k p) e -> p k e", p=128)[:, :, ts(j)]
                    if j == 0 and wi == 0:
                        wx = wx + [A_done_sc, A_done_ve]
                    w_l = p.dma("sync", fs, src, f"pf{wi % 2}", wait=wx + wfs)
                    w_c = p.op("vector", lambda E, fs=fs, dst=dst: E.tensor_copy(dst, fs),
                               wait=[w_l])
                    panf_prev[key] = w_c
                    pan_ready[(wi, j)] = w_c

            prefetch(0)
            w_y_all = []
            w_scan_prev = [None]
            for j in range(NJ):
                if j + 1 < NJ:
                    prefetch(j + 1)
                pin = panW[0 + (j % 2)]
                pdec = panW[2 + (j % 2)]
                pgate = panW[4 + (j % 2)]
                sc_bar = p.bar("scalar")
                w_mm_in = p.op("tensor", lambda E, pin=pin: [
                    E.matmul(pA[:, 512 * t4:512 * (t4 + 1)], pin[:, ts(k)],
                             xn_fm[:, k * TW + 512 * t4:k * TW + 512 * (t4 + 1)],
                             start=(k == 0), stop=(k == NJ - 1))
                    for k in range(NJ) for t4 in range(4)],
                    wait=[pan_ready[(0, j)], sc_bar])
                w_mm_dec = p.op("tensor", lambda E, pdec=pdec: [
                    E.matmul(pB[:, 512 * t4:512 * (t4 + 1)], pdec[:, ts(k)],
                             xn_fm[:, k * TW + 512 * t4:k * TW + 512 * (t4 + 1)],
                             start=(k == 0), stop=(k == NJ - 1))
                    for k in range(NJ) for t4 in range(4)],
                    wait=[pan_ready[(1, j)]])
                ev_in = p.op("scalar", lambda E, j=j: E.activation(
                    out=ibuf, in_=pA[:], func=AF.Silu, bias=bi(0)[:, j:j + 1]),
                    wait=[w_mm_in, w_b1] + ([w_y_all[-1]] if w_y_all else []))
                ev_a = p.op("scalar", lambda E, j=j, w_pv=(w_scan_prev[0] if w_scan_prev[0] else None): E.activation(
                    out=abuf, in_=pB[:], func=AF.Sigmoid, bias=bi(1)[:, j:j + 1]),
                    wait=[w_mm_dec, w_bd] + ([w_scan_prev[0]] if w_scan_prev[0] else []))
                ev_oma = p.op("scalar", lambda E, j=j: E.activation(
                    out=bbuf, in_=pB[:], func=AF.Sigmoid, bias=bi(2)[:, j:j + 1],
                    scale=-1.0), wait=[w_bn])
                w_mm_g = p.op("tensor", lambda E, pgate=pgate: [
                    E.matmul(pA[:, 512 * t4:512 * (t4 + 1)], pgate[:, ts(k)],
                             xn_fm[:, k * TW + TO + 512 * t4:k * TW + TO + 512 * (t4 + 1)],
                             start=(k == 0), stop=(k == NJ - 1))
                    for k in range(NJ) for t4 in range(2)],
                    wait=[pan_ready[(2, j)], ev_in])
                panw_prev[0 + (j % 2)] = ev_in
                panw_prev[2 + (j % 2)] = ev_a
                panw_prev[4 + (j % 2)] = w_mm_g
                w_b = p.op("vector", lambda E: E.tensor_tensor(
                    out=bbuf, in0=bbuf, in1=ibuf, op=ALU.mult),
                    wait=[ev_in, ev_oma, w_bm])
                w_bmk = p.op("vector", lambda E: E.tensor_scalar(
                    out=bbuf[:, 0:TO], in0=bbuf[:, 0:TO], scalar1=stat[:, 7:8],
                    scalar2=None, op0=ALU.mult), wait=[w_b])
                w_scan = p.op("vector", lambda E: E.tensor_tensor_scan(
                    out=ibuf, data0=abuf, data1=bbuf, initial=0.0,
                    op0=ALU.mult, op1=ALU.add), wait=[w_bmk, ev_a])
                w_scan_prev[0] = w_scan
                ev_gate = p.op("scalar", lambda E, j=j: E.activation(
                    out=gbuf, in_=pA[:, 0:TO], func=AF.Sigmoid, bias=bi(3)[:, j:j + 1]),
                    wait=[w_mm_g, w_b3])
                w_y = p.op("vector", lambda E, j=j: E.tensor_tensor(
                    out=y_fm[:, j * TO:(j + 1) * TO], in0=gbuf, in1=ibuf[:, TO:TW],
                    op=ALU.mult), wait=[w_scan, ev_gate])
                w_y_all.append(w_y)

            te_bar_c1 = p.bar("tensor")
            WoutBF = lambda k: xn_fm[:, k * 2048:(k + 1) * 2048]
            bout_t = SCR[:, 4096:6144]
            s_buf2 = SCR[:, 6144:8192]
            ve_bar0 = p.bar("vector")
            w_bt = p.dma("sync", bout_t, bout_tile, "ld_bout", wait=[ve_bar0])
            wo_ready = []
            for k in range(NJ):
                fs = panF[k % 2]
                wfs = [panf_prev[k % 2]] if (k % 2) in panf_prev else []
                w_l = p.dma("sync", fs, w_out[ts(k), :], f"pf{k % 2}", wait=wfs)
                w_c = p.op("vector", lambda E, fs=fs, k=k: E.tensor_copy(
                    WoutBF(k), fs), wait=[w_l] + ([te_bar_c1, w_y_all[-1]] if k < 2 else []))
                panf_prev[k % 2] = w_c
                wo_ready.append(w_c)
            s_dma = []
            for m in range(8):
                ps = pA if m % 2 == 0 else pB
                ve_bar = p.bar("vector")
                sc_bar = p.bar("scalar")
                w_mm_o = p.op("tensor", lambda E, m=m, ps=ps: [
                    E.matmul(ps[:, 512 * t4:512 * (t4 + 1)],
                             y_fm[:, k * TO + m * 128:k * TO + (m + 1) * 128],
                             WoutBF(k)[:, 512 * t4:512 * (t4 + 1)],
                             start=(k == 0), stop=(k == NJ - 1))
                    for k in range(NJ) for t4 in range(4)],
                    wait=[wo_ready[-1], w_y_all[-1], ve_bar, sc_bar])
                w_s = p.op("vector", lambda E, ps=ps: E.tensor_tensor(
                    out=s_buf2, in0=ps[:], in1=bout_t, op=ALU.add),
                    wait=[w_mm_o, w_bt])
                w_sd = p.dma("gpsimd", s_scr[m * 128:(m + 1) * 128, :], s_buf2, "sw",
                             wait=[w_s])
                p.wait_only("vector", [w_sd])
                s_dma.append(w_sd)

        if n_phases >= 3:
            natS = [SCR[:, 0:4096], SCR[:, 4096:8192]]
            deqBB = [[SCR[:, 8192 + 4096 * s + 1024 * r:8192 + 4096 * s + 1024 * (r + 1)].bitcast(BF16)
                      for r in range(4)] for s in range(2)]
            wstg = [SCR[:, 16384 + 256 * i:16384 + 256 * (i + 1)].bitcast(BF16)
                    for i in range(8)]
            qscW = qsc[:]
            qs2W = qs2[:]
            qrW = qrr[:]
            q8W = SCR[:, 18432:18944].bitcast(F8E4)
            nat_prev = {}
            wstg_prev = {}
            deq_guard = {}
            blkcc_i = [0]

            def prep_weight(nat, out, R, C, tag):
                nch = C // 2048
                natv = nat.rearrange("(kk p) c -> p kk c", p=128)
                for blk in range(R // 512):
                    for cc in range(nch):
                        bci = blkcc_i[0]
                        blkcc_i[0] += 1
                        deqB = deqBB[0]
                        deq_ready = []
                        for r in range(4):
                            fs = natS[(r // 2) % 2][:, 2048 * (r % 2):2048 * (r % 2 + 1)]
                            if r % 2 == 0:
                                rp = r // 2
                                slot = rp % 2
                                wx = [nat_prev[slot]] if slot in nat_prev else []
                                if blk == 0 and cc == 0 and tag == "g" and n_phases >= 2:
                                    wx += [p.bar("scalar"), p.bar("vector")]
                                w_l = p.dma("sync", natS[slot],
                                            natv[:, 4 * blk + 2 * rp:4 * blk + 2 * rp + 2,
                                                 cc * 2048:(cc + 1) * 2048],
                                            f"pf{slot}", wait=wx)
                            a3 = fs.rearrange("p (b c) -> p b c", c=32)
                            w_red = p.op("vector", lambda E, a3=a3: E.tensor_reduce(
                                out=qscW, in_=a3, axis=AX.X, op=ALU.max,
                                apply_absolute_value=True), wait=[w_l])
                            w_s2 = p.op("vector", lambda E: E.tensor_scalar(
                                out=qs2W, in0=qscW, scalar1=1.0 / 224.0, scalar2=2e-30,
                                op0=ALU.mult, op1=ALU.max), wait=[w_red])
                            w_r = p.op("vector", lambda E: E.reciprocal(
                                out=qrW, in_=qs2W), wait=[w_s2])
                            rb = qrW.unsqueeze(2).broadcast_to([128, 64, 32])
                            s2b = qs2W.unsqueeze(2).broadcast_to([128, 64, 32])
                            wdq = [deq_guard[0]] if (r == 0 and 0 in deq_guard) else []
                            w_q = p.op("vector", lambda E, a3=a3, rb=rb: E.tensor_tensor(
                                out=q8W.rearrange("p (b c) -> p b c", c=32), in0=a3,
                                in1=rb, op=ALU.mult), wait=[w_r] + wdq)
                            if r % 2 == 1:
                                nat_prev[(r // 2) % 2] = w_q
                            w_dq = p.op("vector", lambda E, r=r, s2b=s2b: E.tensor_tensor(
                                out=deqB[r].rearrange("p (b c) -> p b c", c=32),
                                in0=q8W.rearrange("p (b c) -> p b c", c=32), in1=s2b,
                                op=ALU.mult), wait=[w_q])
                            deq_ready.append(w_dq)
                        for hc in range(16):
                            pt = pTa if hc % 2 == 0 else pTb
                            sc_bar = p.bar("scalar")
                            w_tr = p.op("tensor", lambda E, hc=hc, pt=pt, deqB=deqB: [
                                E.transpose(pt[:, 128 * r:128 * (r + 1)],
                                            deqB[r][:, ts(hc)], idt[:])
                                for r in range(4)], wait=deq_ready + [sc_bar])
                            wg = [wstg_prev[hc % 8]] if (hc % 8) in wstg_prev else []
                            w_ev = p.op("scalar", lambda E, hc=hc, pt=pt: E.copy(
                                wstg[hc % 8][:], pt[:, 0:512]), wait=[w_tr] + wg)
                            w_wr = p.dma("gpsimd", out[cc * 2048 + hc * 128:cc * 2048 + (hc + 1) * 128,
                                                      blk * 512:(blk + 1) * 512],
                                         wstg[hc % 8][:], f"ws{hc % 8}", wait=[w_ev])
                            wstg_prev[hc % 8] = w_wr
                        deq_guard[0] = w_tr
            prep_weight(gw_nat, gwq, II, D, "g")
            prep_weight(uw_nat, uwq, II, D, "u")
            prep_weight(dw_nat, dwq, D, II, "d")
            W_done = [p.total(f"ws{i}") for i in range(8)]

        if n_phases >= 4:
            panE = [y_fm[:, 0:8192], y_fm[:, 8192:16384]]
            panE2 = [SCR[:, 1600 + 4096 * i:1600 + 4096 * (i + 1)].bitcast(BF16)
                     for i in range(2)]
            gsil = SCR[:, 0:256].bitcast(BF16)
            hbuf = SCR[:, 256:768]
            q8e2 = [SCR[:, 768 + 128 * i:768 + 128 * (i + 1)].bitcast(F8E4)
                    for i in range(2)]
            hqb2 = [SCR[:, 1024 + 256 * i:1024 + 256 * (i + 1)].bitcast(BF16)
                    for i in range(2)]
            qscE = SCR[:, 1536:1552]
            qs2E = SCR[:, 1552:1568]
            qrE = SCR[:, 1568:1584]
            panE_prev = {}
            dpanF = [SCR[:, 9792 + 2048 * i:9792 + 2048 * (i + 1)].bitcast(BF16)
                     for i in range(3)]
            gfs2 = [SCR[:, 15936:16960], SCR[:, 16960:17984]]
            sfs = SCR[:, 17984:19008]
            dpan_prev = {}
            gfs_prev = {}

            def pan_pair(i5):
                return (panE[0], panE[1]) if i5 % 2 == 0 else (panE2[0], panE2[1])

            def prefetch_e(th, i5):
                gp, up = pan_pair(i5)
                key = i5 % 2
                wx = [panE_prev[key]] if key in panE_prev else []
                if th == 0 and i5 < 2:
                    wx = wx + [p.bar("scalar"), p.bar("vector")] + W_done
                w_g = p.dma("sync", gp, gwq.rearrange("(k p) i -> p k i", p=128)[:, :, 512 * i5:512 * (i5 + 1)],
                            f"eg{i5 % 2}", wait=wx)
                w_u = p.dma("sync", up, uwq.rearrange("(k p) i -> p k i", p=128)[:, :, 512 * i5:512 * (i5 + 1)],
                            f"eu{i5 % 2}", wait=[])
                return (w_g, w_u)

            hq_evs = []
            hq_tr_prev = [None]
            pend_tr = [None]
            for th in range(2):
                pr = prefetch_e(th, 0)
                e_ready = {0: pr}
                mm_prev = {}
                for i5 in range(16):
                    if i5 + 1 < 16:
                        e_ready[i5 + 1] = prefetch_e(th, i5 + 1)
                    gp, up = pan_pair(i5)
                    for m in range(4):
                        half = (m % 2)
                        psg = pA[:, 1024 * half:1024 * half + 512]
                        psu = pA[:, 1024 * half + 512:1024 * half + 1024]
                        sc_bar = p.bar("scalar")
                        ve_bar = p.bar("vector")
                        w_mm = p.op("tensor", lambda E, gp=gp, up=up, th=th, m=m, psg=psg, psu=psu: [
                            E.matmul(pg_ps, lhsT, rhs, start=(k == 0), stop=(k == NJ - 1))
                            for k in range(NJ)
                            for (pg_ps, lhsT, rhs) in (
                                (psg, xq_fm[:, k * TO + th * 512 + m * 128:k * TO + th * 512 + (m + 1) * 128],
                                 gp[:, 512 * k:512 * (k + 1)]),
                                (psu, xq_fm[:, k * TO + th * 512 + m * 128:k * TO + th * 512 + (m + 1) * 128],
                                 up[:, 512 * k:512 * (k + 1)]),
                            )],
                            wait=list(e_ready[i5]) + [sc_bar, ve_bar])
                        mm_prev[half] = w_mm
                        if pend_tr[0] is not None:
                            pend_tr[0]()
                            pend_tr[0] = None
                        ev_sil = p.op("scalar", lambda E, psg=psg: E.activation(
                            out=gsil, in_=psg, func=AF.Silu), wait=[w_mm])
                        w_h = p.op("vector", lambda E, psu=psu: E.tensor_tensor(
                            out=hbuf, in0=gsil, in1=psu, op=ALU.mult), wait=[ev_sil])
                        a3 = hbuf.rearrange("p (b c) -> p b c", c=32)
                        w_red = p.op("vector", lambda E, a3=a3: E.tensor_reduce(
                            out=qscE, in_=a3, axis=AX.X, op=ALU.max,
                            apply_absolute_value=True), wait=[w_h])
                        w_s2 = p.op("vector", lambda E: E.tensor_scalar(
                            out=qs2E, in0=qscE, scalar1=1.0 / 224.0, scalar2=2e-30,
                            op0=ALU.mult, op1=ALU.max), wait=[w_red])
                        w_r = p.op("vector", lambda E: E.reciprocal(
                            out=qrE, in_=qs2E), wait=[w_s2])
                        rb = qrE.unsqueeze(2).broadcast_to([128, 16, 32])
                        s2b = qs2E.unsqueeze(2).broadcast_to([128, 16, 32])
                        gpar = (i5 * 4 + m) % 2
                        q8e = q8e2[gpar]
                        hqb = hqb2[gpar]
                        w_q = p.op("vector", lambda E, a3=a3, rb=rb, q8e=q8e: E.tensor_tensor(
                            out=q8e.rearrange("p (b c) -> p b c", c=32), in0=a3,
                            in1=rb, op=ALU.mult),
                            wait=[w_r] + ([hq_tr_prev[0]] if hq_tr_prev[0] else []))
                        w_dq = p.op("vector", lambda E, s2b=s2b, q8e=q8e, hqb=hqb: E.tensor_tensor(
                            out=hqb.rearrange("p (b c) -> p b c", c=32),
                            in0=q8e.rearrange("p (b c) -> p b c", c=32), in1=s2b,
                            op=ALU.mult), wait=[w_q])

                        def emit_tr(i5=i5, m=m, th=th, w_dq=w_dq, hqb=hqb, gpar=gpar):
                            pt = pTa if gpar == 0 else pTb
                            sc_bar2 = p.bar("scalar")
                            w_tr = p.op("tensor", lambda E, pt=pt, hqb=hqb: [
                                E.transpose(pt[:, 128 * t:128 * (t + 1)], hqb[:, ts(t)], idt[:])
                                for t in range(4)], wait=[w_dq, sc_bar2])
                            hq_tr_prev[0] = w_tr
                            w_ev = p.op("scalar", lambda E, i5=i5, m=m, pt=pt: [
                                E.copy(hq(i5 * 4 + t)[:, m * 128:(m + 1) * 128],
                                       pt[:, 128 * t:128 * (t + 1)]) for t in range(4)],
                                wait=[w_tr])
                            hq_evs.append(w_ev)
                        pend_tr[0] = emit_tr
                    panE_prev[i5 % 2] = hq_evs[-1] if hq_evs else mm_prev[1]
                if pend_tr[0] is not None:
                    pend_tr[0]()
                    pend_tr[0] = None
                E_done = [p.bar("scalar"), p.bar("vector"), p.bar("tensor")]

                if n_phases >= 5:
                    dwqv = dwq.rearrange("(kk p) h -> p kk h", p=128)
                    for Hh in range(2):
                        mm_last = None
                        for kb in range(16):
                            fs = dpanF[kb % 3]
                            wx = [dpan_prev[kb % 3]] if (kb % 3) in dpan_prev else []
                            if th == 0 and Hh == 0 and kb < 2:
                                wx = wx + E_done + [p.total("sw")]
                            w_l = p.dma("sync", fs,
                                        dwqv[:, 4 * kb:4 * kb + 4,
                                             1024 * Hh:1024 * (Hh + 1)],
                                        f"dp{kb % 3}", wait=wx)
                            bars = ([p.bar("vector"), p.bar("scalar")] if kb == 0 else [])
                            w_mm = p.op("tensor", lambda E, fs=fs, kb=kb, th=th: [
                                E.matmul((pA if m < 2 else pB)[:, 1024 * (m % 2) + 512 * t2:
                                                               1024 * (m % 2) + 512 * (t2 + 1)],
                                         hq(4 * kb + ks)[:, m * 128:(m + 1) * 128],
                                         fs[:, 1024 * ks + 512 * t2:1024 * ks + 512 * (t2 + 1)],
                                         start=(kb == 0 and ks == 0), stop=(kb == 15 and ks == 3))
                                for ks in range(4) for m in range(4) for t2 in range(2)],
                                wait=[w_l, hq_evs[-1]] + bars)
                            dpan_prev[kb % 3] = w_mm
                            mm_last = w_mm
                        for m in range(4):
                            mg = th * 4 + m
                            gi = m % 2
                            pset = (pA if m < 2 else pB)[:, 1024 * (m % 2):1024 * (m % 2 + 1)]
                            gbuf2 = gfs2[gi]
                            wg_re = [gfs_prev[gi]] if gi in gfs_prev else []
                            ev_g = p.op("scalar", lambda E, pset=pset, gbuf2=gbuf2: E.activation(
                                out=gbuf2, in_=pset, func=AF.Silu),
                                wait=[mm_last] + wg_re)
                            w_ls = p.dma("sync", sfs,
                                         s_scr[mg * 128:(mg + 1) * 128,
                                               1024 * Hh:1024 * (Hh + 1)],
                                         "fs", wait=[p.total("sw")])
                            w_t = p.op("vector", lambda E, gbuf2=gbuf2: E.tensor_tensor(
                                out=gbuf2, in0=sfs, in1=gbuf2, op=ALU.mult),
                                wait=[ev_g, w_ls])
                            p.wait_only("sync", [w_t])
                            w_xa = p.dma("gpsimd", gbuf2,
                                         x_win[TO + mg * 128:TO + (mg + 1) * 128,
                                               1024 * Hh:1024 * (Hh + 1)],
                                         f"xa{gi}", wait=[w_t], accum=ALU.add)
                            w_od = p.dma("gpsimd", out_tm[mg * 128:(mg + 1) * 128,
                                                          1024 * Hh:1024 * (Hh + 1)],
                                         gbuf2, f"ow{gi}", wait=[w_xa])
                            gfs_prev[gi] = w_od

        if debug:
            dcp = SCR[:, 0:2048]
            if n_phases < 3:
                for j in range(NJ):
                    w_c = p.op("vector", lambda E, j=j: E.tensor_copy(
                        dcp, xn_fm[:, j * TW:(j + 1) * TW]), wait=[tr_done[-1], p.bar("scalar")])
                    w_d = p.dma("gpsimd", dbg["xn_fm"][j], dcp, "gd", wait=[w_c])
                    p.wait_only("vector", [w_d])
                for j in range(NJ):
                    w_c = p.op("vector", lambda E, j=j: E.tensor_copy(
                        dcp[:, 0:TO], xq_fm[:, j * TO:(j + 1) * TO]), wait=[trq_done[-1]])
                    w_d = p.dma("gpsimd", dbg["xq_fm"][j], dcp[:, 0:TO], "gd", wait=[w_c])
                    p.wait_only("vector", [w_d])
            if n_phases >= 2:
                w_c2 = p.dma("gpsimd", dbg["s_fm"], s_scr[:, :], "gd", wait=[s_dma[-1]])
            if n_phases == 3:
                p.dma("gpsimd", dbg["gwq"], gwq[0:256, :], "gd", wait=W_done)
            if n_phases >= 5:
                p.dma("gpsimd", dbg["dwq"], dwq[0:256, :], "gd", wait=W_done)
            if n_phases == 4:
                for c in range(NIC):
                    w_c = p.op("vector", lambda E, c=c: E.tensor_copy(
                        dcp[:, 0:512], hq(c)), wait=[hq_evs[-1]])
                    w_d = p.dma("gpsimd", dbg["hq"][c][:, 0:512], dcp[:, 0:512], "gd", wait=[w_c])
                    p.wait_only("vector", [w_d])

        fin = []
        for nm in ("ow0", "ow1", "gd"):
            if nm in p.sems:
                fin.append(p.total(nm))
        p.wait_only("gpsimd", fin)
        with nc.Block() as block:
            p.install(block)
    return nc




_CACHE = {}


def kernel(L0_attn_x, L0_attn_W_in, L0_attn_b_in, L0_attn_W_decay, L0_attn_b_decay,
           L0_attn_W_gate, L0_attn_b_gate, L0_attn_W_out, L0_attn_b_out,
           L0_attn_decay_bias, L0_ffn_gate_w, L0_ffn_up_w, L0_ffn_down_w):
    import ml_dtypes
    x = np.ascontiguousarray(np.asarray(L0_attn_x, dtype=np.float32))
    B = x.shape[0]
    ident = np.eye(128, dtype=ml_dtypes.bfloat16)
    bout_tile = np.ascontiguousarray(
        np.broadcast_to(np.asarray(L0_attn_b_out, np.float32), (128, D)))
    zeros_ctx = np.zeros((TO, D), np.float32)
    shared = {
        "ident": ident,
        "bout_tile": bout_tile,
        "w_in": np.ascontiguousarray(L0_attn_W_in, np.float32),
        "w_dec": np.ascontiguousarray(L0_attn_W_decay, np.float32),
        "w_gate": np.ascontiguousarray(L0_attn_W_gate, np.float32),
        "w_out": np.ascontiguousarray(L0_attn_W_out, np.float32),
        "b_in": np.ascontiguousarray(L0_attn_b_in, np.float32),
        "b_dec": np.ascontiguousarray(L0_attn_b_decay, np.float32),
        "b_gate": np.ascontiguousarray(L0_attn_b_gate, np.float32),
        "b_out": np.ascontiguousarray(L0_attn_b_out, np.float32),
        "dec_bias": np.ascontiguousarray(L0_attn_decay_bias, np.float32),
        "gw_nat": np.ascontiguousarray(L0_ffn_gate_w, np.float32),
        "uw_nat": np.ascontiguousarray(L0_ffn_up_w, np.float32),
        "dw_nat": np.ascontiguousarray(L0_ffn_down_w, np.float32),
    }
    in_maps = []
    for c in range(8):
        b, half = c // 2, c % 2
        if half == 0:
            x_win = np.concatenate([zeros_ctx, x[b, 0:TO]], axis=0)
        else:
            x_win = x[b]
        m = dict(shared)
        m["x_win"] = np.ascontiguousarray(x_win)
        m["bmask"] = np.full((128, 1), float(half), np.float32)
        in_maps.append(m)
    if "nc" not in _CACHE:
        _CACHE["nc"] = build()
    res = run_bass_kernel_spmd(_CACHE["nc"], in_maps, core_ids=list(range(8)))
    out = np.empty((B, 2048, D), np.float32)
    for c in range(8):
        b, half = c // 2, c % 2
        out[b, half * TO:(half + 1) * TO] = res.results[c]["out_tm"]
    return out
